# version 38
# baseline (speedup 1.0000x reference)
import numpy as np
import concourse.bass as bass
import concourse.mybir as mybir
from concourse.bass_utils import run_bass_kernel_spmd
from contextlib import ExitStack

F32 = mybir.dt.float32
BF16 = mybir.dt.bfloat16
AF = mybir.ActivationFunctionType
ALU = mybir.AluOpType
AX = mybir.AxisListType

NSEQ, S, DM, NH, DH, NHP, DI = 2, 2048, 1024, 16, 64, 8, 2816
NT, NQT, BLK, EPS = 16, 4, 16, 1e-5

XT = 0
QT = 16384
KT = 32768
VAUG = 49152
KST = 65792
VSAUG = 69888
XBF = 74048
EXPS = 74048
EXPL = 80448
EXPL_STRIDE = 2304
P1T = 77376
WQKV = 80448
STG = 88640
A_COLS = 89152
WGT = 0
WUT = 22528
WDT = 45056
Y1Q = 67584
Y1QT = 69632
SGN = 71680
CCH = 73088
SGP = 74496
HQ = 75008
HQT = 80640
SGNY = 83456
VB2 = 84480
CY1B = 85888
VBY = 86912
SGP2 = 87936
SGP3 = 88640

ZSTG = 0
ZZ = 2048
RZP = 2304
RZR = 0
Y1P = 2560
HH = 3584
QS = 4992
WRAW = 5344
P1S = 6368
P1A = HH
HHY = 6496
Z_COLS = 7520


class Sem:
    def __init__(self, h):
        self.h = h
        self.n = 0


class E:
    def __init__(self, eng):
        self.e = eng
        self.wr = []
        self.rd = []
        self.waits = {}

    def wait(self, sem, val):
        if val <= 0 or self.waits.get(id(sem), -1) >= val:
            return
        self.e.wait_ge(sem.h, val)
        self.waits[id(sem)] = val

    def drain(self):
        self.e.drain()
        self.wr = []
        self.rd = []

    @staticmethod
    def _rng(a):
        off = a.offset - a.base_partition() * a.ap[0][0]
        span = 1
        for st, n in a.ap[1:]:
            span = max(span, abs(st) * (n - 1) + 1)
        es = 2 if a.dtype == BF16 else 4
        return (a.tensor.name, off * es, (off + span) * es)

    @staticmethod
    def _hit(r, s):
        return r[0] == s[0] and r[1] < s[2] and s[1] < r[2]

    def _pre(self, outs, ins):
        no = [self._rng(a) for a in outs]
        ni = [self._rng(a) for a in ins]
        conflict = any(self._hit(r, s) for r in ni for s in self.wr) or \
            any(self._hit(r, s) for r in no for s in self.wr) or \
            any(self._hit(r, s) for r in no for s in self.rd)
        if conflict:
            self.drain()
        self.wr.extend(no)
        self.rd.extend(ni)

    def cp(self, out, in_):
        self._pre([out], [in_])
        if hasattr(self.e, "tensor_copy"):
            return self.e.tensor_copy(out, in_)
        return self.e.copy(out, in_)

    def ts(self, out, in0, s1, s2=None, op0=ALU.mult, op1=ALU.bypass):
        ins = [in0] + [x for x in (s1, s2) if isinstance(x, bass.AP)]
        self._pre([out], ins)
        if s2 is None:
            return self.e.tensor_single_scalar(out, in0, s1, op=op0)
        return self.e.tensor_scalar(out, in0, s1, s2, op0=op0, op1=op1)

    def tt(self, out, a, b, op):
        self._pre([out], [a, b])
        return self.e.tensor_tensor(out, a, b, op=op)

    def stt(self, out, in0, scalar, in1, op0, op1):
        self._pre([out], [in0, in1])
        return self.e.scalar_tensor_tensor(out, in0, scalar, in1, op0=op0, op1=op1)

    def red(self, out, in_, op=ALU.max, absval=False):
        self._pre([out], [in_])
        return self.e.tensor_reduce(out, in_, axis=AX.X, op=op,
                                    apply_absolute_value=absval)

    def recip(self, out, in_):
        self._pre([out], [in_])
        return self.e.reciprocal(out, in_)

    def mset(self, ap, c):
        self._pre([ap], [])
        return self.e.memset(ap, c)

    def act(self, out, in_, f, bias=0.0, scale=1.0, accum=None):
        ins = [in_] + [x for x in (bias, scale) if isinstance(x, bass.AP)]
        self._pre([out] + ([accum] if accum is not None else []), ins)
        return self.e.activation(out, in_, f, bias=bias, scale=scale,
                                 accum_out=accum)

    def bnstats(self, out, in_):
        self._pre([out], [in_])
        return self.e.bn_stats(out, in_)

    def bnaggr(self, out, in_):
        self._pre([out], [in_])
        return self.e.bn_aggr(out, in_)

    def iota(self, out, pattern, base=0, chm=0):
        self._pre([out], [])
        return self.e.iota(out, pattern=pattern, base=base,
                           channel_multiplier=chm,
                           allow_small_or_imprecise_dtypes=True)

    def aff(self, out, in_, cmp, fill, base, pattern, chm):
        self._pre([out], [in_])
        return self.e.affine_select(out=out, in_=in_, compare_op=cmp, fill=fill,
                                    base=base, pattern=pattern,
                                    channel_multiplier=chm)

    def dma(self, out, in_):
        return self.e.dma_start(out=out, in_=in_)

    def dmat(self, out, in_):
        return self.e.dma_start_transpose(out, in_)

    def mm(self, out, lhsT, rhs, start=True, stop=True):
        return self.e.matmul(out, lhsT, rhs, start=start, stop=stop)

    def nop(self):
        return self.e.engine_nop()


def build(nc, dbg_keys=(), upto="g", seqs=NSEQ, with_p1=True):
    dbg = {}
    x_h = nc.dram_tensor("x", [NSEQ, S, DM], F32, kind="ExternalInput").ap()
    qw_h = nc.dram_tensor("qw", [DM, DM], F32, kind="ExternalInput").ap()
    kw_h = nc.dram_tensor("kw", [DM, DM], F32, kind="ExternalInput").ap()
    vw_h = nc.dram_tensor("vw", [DM, DM], F32, kind="ExternalInput").ap()
    ow_h = nc.dram_tensor("ow", [DM, DM], F32, kind="ExternalInput").ap()
    ob_h = nc.dram_tensor("ob", [DM], F32, kind="ExternalInput").ap()
    us_h = nc.dram_tensor("us", [1], F32, kind="ExternalInput").ap()
    wg_h = nc.dram_tensor("wg", [DI, DM], F32, kind="ExternalInput").ap()
    wu_h = nc.dram_tensor("wu", [DI, DM], F32, kind="ExternalInput").ap()
    wd_h = nc.dram_tensor("wd", [DM, DI], F32, kind="ExternalInput").ap()
    y_h = nc.dram_tensor("y", [NSEQ, S, DM], F32, kind="ExternalOutput").ap()
    wat_h = nc.dram_tensor("wattn_s", [4, DM, DM], BF16, kind="Internal").ap()
    wgq_h = nc.dram_tensor("wgq_s", [DI, DM], BF16, kind="Internal").ap()
    wuq_h = nc.dram_tensor("wuq_s", [DI, DM], BF16, kind="Internal").ap()
    wdq_h = nc.dram_tensor("wdq_s", [DM, DI], BF16, kind="Internal").ap()
    y1_h = nc.dram_tensor("y1_s", [NSEQ, S, DM], F32, kind="Internal").ap()
    for k in dbg_keys:
        shp = {"xT": [128, 16384], "qT": [128, 16384], "kT": [128, 16384],
               "vaug": [128, 16640], "kst": [128, 4096], "vsaug": [128, 4160],
               "att": [128, 16384], "zz": [128, 256], "expl": [128, 2304],
               "y1q": [128, 1024], "h0": [128, 1408], "hq": [128, 2816]}[k]
        dbg[k] = nc.dram_tensor("dbg_" + k, shp, BF16 if k not in ("zz", "h0")
                                else F32, kind="ExternalOutput").ap()

    es = ExitStack()
    A_t = es.enter_context(nc.sbuf_tensor("A", [128, A_COLS], BF16))
    Z_t = es.enter_context(nc.sbuf_tensor("Zf", [128, Z_COLS], F32))
    ident = es.enter_context(nc.sbuf_tensor("ident", [128, 128], BF16))
    maskl = es.enter_context(nc.sbuf_tensor("maskl", [128, 144], BF16))
    csml = es.enter_context(nc.sbuf_tensor("csml", [128, 4], F32))
    lns = es.enter_context(nc.sbuf_tensor("lns", [128, 20], F32))
    ones1 = es.enter_context(nc.sbuf_tensor("ones1", [1, 128], BF16))
    onesf = es.enter_context(nc.sbuf_tensor("onesf", [1, 128], F32))
    ones2f = es.enter_context(nc.sbuf_tensor("ones2f", [65, 128], F32))
    obbf = es.enter_context(nc.sbuf_tensor("obbf", [1, 1024], BF16))
    ussb = es.enter_context(nc.sbuf_tensor("ussb", [1, 1], F32))
    P = [es.enter_context(nc.psum_tensor("P%d" % i, [128, 512], F32))
         for i in range(8)]
    ch_names = ("sT", "sV", "sA", "sG", "cOB", "cWR", "cWS", "cWQ",
                "cXB0", "cXB1", "cX0", "cX1", "cSTG", "cZ", "cWT",
                "cY1", "cY", "cDBG", "cPS", "cZU0", "cZU1")
    sem_h = {k: es.enter_context(nc.semaphore(k)) for k in ch_names}
    (sT, sV, sA, sG, cOB, cWR, cWS, cWQ, cXB0, cXB1, cX0, cX1, cSTG, cZ,
     cWT, cY1, cY, cDBG, cPS, cZU0, cZU1) = (Sem(sem_h[k]) for k in ch_names)
    cZU = (cZU0, cZU1)
    cXB = (cXB0, cXB1)
    cX = (cX0, cX1)

    rec = {k: [] for k in ("sync", "tensor", "vector", "scalar", "gpsimd")}

    def w8(eng, sem, val=None):
        v = sem.n if val is None else val
        rec[eng].append(lambda w, s=sem, v=v: w.wait(s, v))

    def run(eng, fn, inc=None, k=1):
        if inc is not None:
            inc.n += k
        def thunk(w, fn=fn, inc=inc, k=k):
            ins = fn(w)
            if inc is not None:
                ins.then_inc(inc.h, k)
        rec[eng].append(thunk)

    bank_ev = {i: (sV, 0) for i in range(8)}

    def bank_wait(i, eng="tensor"):
        s, v = bank_ev[i]
        w8(eng, s, v)

    def bank_set(i, sem):
        bank_ev[i] = (sem, sem.n)

    Ab = lambda off, wdt: A_t[:, off:off + wdt]
    Zp = lambda off, wdt: Z_t[:, off:off + wdt]

    def blk3(ap):
        return ap.rearrange("p (nb blk) -> p nb blk", blk=BLK)

    def _mk3(t, base, n, step, wdt, lo=0):
        v = t[:, base:base + n * step]
        v = v.rearrange("p (c n) -> p c n", c=n)
        return v[:, :, lo:lo + wdt]

    run("vector", lambda w: w.mset(ident[:], 0.0))
    run("vector", lambda w: w.mset(maskl[:], 1.0))
    run("vector", lambda w: w.mset(ones1[:], 1.0))
    run("vector", lambda w: w.mset(onesf[:], 1.0))
    run("vector", lambda w: w.mset(ones2f[:], 1.0))
    run("vector", lambda w: w.mset(ident[:1, :1], 0.0), inc=sV)
    w8("gpsimd", sV)
    run("gpsimd", lambda w: w.aff(ident[:], ident[:], ALU.not_equal, 1.0, 0,
                                  [[-1, 128]], 1))
    run("gpsimd", lambda w: w.aff(maskl[:], maskl[:], ALU.is_ge, 0.0, 0,
                                  [[1, 144]], -1))
    run("gpsimd", lambda w: w.aff(maskl[:], maskl[:], ALU.is_ge, 0.0, 16,
                                  [[-1, 144]], 1))
    run("gpsimd", lambda w: w.iota(
        Z_t[0:1, ZZ:ZZ + 128].rearrange("o (g r) -> o g r", r=4),
        [[0, 32], [1, 4]], 0, 0))
    run("gpsimd", lambda w: w.ts(Z_t[0:1, ZZ:ZZ + 128], Z_t[0:1, ZZ:ZZ + 128],
                                 0.0, None, op0=ALU.is_gt), inc=sG)
    CONST_G0 = sG.n
    run("sync", lambda w: w.dma(Z_t[0:1, Y1P:Y1P + 1024], ob_h.unsqueeze(0)),
        inc=cOB, k=16)
    run("sync", lambda w: w.dma(ussb[:], us_h.unsqueeze(0)), inc=cOB, k=16)
    w8("vector", cOB)
    run("vector", lambda w: w.cp(obbf[:], Z_t[0:1, Y1P:Y1P + 1024]), inc=sV)
    w8("tensor", cOB)
    w8("tensor", sV)
    run("tensor", lambda w: w.mm(P[0][:, 0:1], onesf[:], ussb[:]), inc=sT)
    w8("scalar", sT)
    run("scalar", lambda w: w.act(csml[:, 0:1], P[0][:, 0:1], AF.Sigmoid), inc=sA)
    w8("vector", sA)
    run("vector", lambda w: w.ts(csml[:, 0:1], csml[:, 0:1], 0.125), inc=sV)
    w8("tensor", sG, CONST_G0)
    w8("tensor", sA)
    run("tensor", lambda w: w.mm(P[0][:, 2:3], Z_t[0:1, ZZ:ZZ + 128],
                                 onesf[0:1, 0:1]), inc=sT)
    w8("vector", sT)
    run("vector", lambda w: w.cp(csml[:, 1:2], P[0][:, 2:3]), inc=sV)
    w8("gpsimd", sV)
    run("gpsimd", lambda w: w.ts(maskl[:], maskl[:], csml[:, 1:2], op0=ALU.mult),
        inc=sG)
    CONST_G = sG.n
    bank_set(0, sV)
    sig8 = csml[:, 0:1]

    for wi, wh in enumerate((qw_h, kw_h, vw_h, ow_h)):
        for it in range(8):
            w8("gpsimd", sV)
            run("gpsimd", lambda w, wh=wh, it=it:
                w.dma(Zp(WRAW, 1024), wh[it * 128:(it + 1) * 128, :]),
                inc=cWR, k=16)
            w8("vector", cWR)
            w8("vector", cPS)
            run("vector", lambda w: w.cp(Ab(P1T, 1024), Zp(WRAW, 1024)), inc=sV)
            w8("gpsimd", sV)
            run("gpsimd", lambda w, wi=wi, it=it:
                w.dma(wat_h[wi, it * 128:(it + 1) * 128, :], Ab(P1T, 1024)),
                inc=cPS, k=16)

    p1_jobs = []
    for src_, dst_ in ((wg_h, wgq_h), (wu_h, wuq_h)):
        for it in range(DI // 128):
            p1_jobs.append((src_, dst_, it, 0, 1024))
    for it in range(DM // 128):
        for off in (0, 1024, 2048):
            p1_jobs.append((wd_h, wdq_h, it, off, min(1024, DI - off)))
    P1_TOTAL = 16 * len(p1_jobs)
    p1_idx = [0]

    p1_pref = [0]
    p1_sA = [0, 0]
    p1_sV = [0, 0]

    def p1_load(i):
        if i >= len(p1_jobs):
            return
        src_, dst_, it, off, pw = p1_jobs[i]
        w8("gpsimd", sA)
        run("gpsimd", lambda w, s=src_, it=it, o=off, pw=pw:
            w.dma(Zp(WRAW, pw), s[it * 128:(it + 1) * 128, o:o + pw]),
            inc=cWR, k=16)
        p1_pref[0] = i + 1

    p1_cap = [len(p1_jobs)]
    p1_phase = [0]

    def p1_tick(n=1):
        for _ in range(n):
            if p1_idx[0] >= min(p1_cap[0], len(p1_jobs)):
                return
            i = p1_idx[0]
            src_, dst_, it, off, pw = p1_jobs[i]
            if p1_phase[0] == 0:
                if p1_pref[0] <= i:
                    p1_load(i)
                w8("scalar", cWR)
                w8("scalar", sV)
                run("scalar", lambda w, pw=pw:
                    w.act(Ab(P1T + 2048, pw), Zp(WRAW, pw), AF.Sign))
                run("scalar", lambda w, pw=pw:
                    w.act(Zp(P1A, pw), Zp(WRAW, pw), AF.Abs), inc=sA)
                w8("vector", sA)
                w8("vector", cWS)
                w8("vector", cPS)
                run("vector", lambda w, pw=pw:
                    quant_v(w, Zp(P1A, pw), Ab(P1T, pw), pw, c_off=P1T + 1024,
                            sgn_ap=Ab(P1T + 2048, pw), qs_off=P1S, vb_off=-1,
                            part=0), inc=sV)
                p1_phase[0] = 1
            else:
                run("vector", lambda w, pw=pw:
                    quant_v(w, Zp(P1A, pw), Ab(P1T, pw), pw, c_off=P1T + 1024,
                            sgn_ap=Ab(P1T + 2048, pw), qs_off=P1S, vb_off=-1,
                            part=1), inc=sV)
                p1_sV[0] = sV.n
                p1_load(i + 1)
                w8("gpsimd", sV, p1_sV[0])
                run("gpsimd", lambda w, d=dst_, it=it, o=off, pw=pw:
                    w.dma(d[it * 128:(it + 1) * 128, o:o + pw], Ab(P1T, pw)),
                    inc=cWS, k=16)
                p1_phase[0] = 0
                p1_idx[0] += 1

    def quant_v(w, sf, dst_bf, pw, c_off=None, sgn_ap=None, qs_off=None,
                vb_off=None, part=None):
        nb = pw // BLK
        qs = QS if qs_off is None else qs_off
        nb_stride = 176 if qs_off is None else 64
        bmax = Zp(qs, nb)
        sv = Zp(qs + nb_stride, nb)
        c = Ab(CCH if c_off is None else c_off, pw)
        sgn = Ab(SGN, pw) if sgn_ap is None else sgn_ap
        vb = (sf if vb_off == -1
              else Ab(VB2 if vb_off is None else vb_off, pw))
        if part != 1:
            w.red(bmax.unsqueeze(2), blk3(sf), op=ALU.max, absval=True)
            w.ts(sv, bmax, 1.0 / 6.0, 1e-12, op0=ALU.mult, op1=ALU.max)
            w.tt(blk3(sgn), blk3(sgn),
                 sv.unsqueeze(2).broadcast_to([128, nb, BLK]),
                 op=ALU.mult)
            w.recip(sv, sv)
            sbc = sv.unsqueeze(2).broadcast_to([128, nb, BLK])
            r = w.tt(blk3(vb), blk3(sf), sbc, op=ALU.mult)
            if part == 0:
                return r
        w.ts(c, vb, 5.0, 2.0, op0=ALU.is_gt, op1=ALU.mult)
        w.ts(dst_bf, vb, 0.25, None, op0=ALU.is_gt)
        w.stt(c, vb, 2.5, c, op0=ALU.is_gt, op1=ALU.add)
        w.stt(dst_bf, vb, 0.75, dst_bf, op0=ALU.is_gt, op1=ALU.add)
        w.stt(c, vb, 3.5, c, op0=ALU.is_gt, op1=ALU.add)
        w.stt(dst_bf, vb, 1.25, dst_bf, op0=ALU.is_gt, op1=ALU.add)
        w.stt(dst_bf, vb, 1.75, dst_bf, op0=ALU.is_gt, op1=ALU.add)
        w.stt(dst_bf, dst_bf, 0.5, c, op0=ALU.mult, op1=ALU.add)
        return w.tt(dst_bf, dst_bf, sgn, op=ALU.mult)

    def qjob_steps(sf, dst_bf, pw, c, sgn, bmax, sv, vb, pre_red=False):
        nb = pw // BLK
        sbc = sv.unsqueeze(2).broadcast_to([128, nb, BLK])
        return ([] if pre_red else [
            lambda w: w.red(bmax.unsqueeze(2), blk3(sf), op=ALU.max,
                            absval=True)]) + [
            lambda w: w.ts(sv, bmax, 1.0 / 6.0, 1e-12, op0=ALU.mult,
                           op1=ALU.max),
            lambda w: w.tt(blk3(sgn), blk3(sgn), sbc, op=ALU.mult),
            lambda w: w.recip(sv, sv),
            lambda w: w.tt(blk3(vb), blk3(sf), sbc, op=ALU.mult),
            lambda w: w.ts(c, vb, 5.0, 2.0, op0=ALU.is_gt, op1=ALU.mult),
            lambda w: w.ts(dst_bf, vb, 0.25, None, op0=ALU.is_gt),
            lambda w: w.stt(c, vb, 2.5, c, op0=ALU.is_gt, op1=ALU.add),
            lambda w: w.stt(dst_bf, vb, 0.75, dst_bf, op0=ALU.is_gt,
                            op1=ALU.add),
            lambda w: w.stt(c, vb, 3.5, c, op0=ALU.is_gt, op1=ALU.add),
            lambda w: w.stt(dst_bf, vb, 1.25, dst_bf, op0=ALU.is_gt,
                            op1=ALU.add),
            lambda w: w.stt(dst_bf, vb, 1.75, dst_bf, op0=ALU.is_gt,
                            op1=ALU.add),
            lambda w: w.stt(dst_bf, dst_bf, 0.5, c, op0=ALU.mult, op1=ALU.add),
            lambda w: w.tt(dst_bf, dst_bf, sgn, op=ALU.mult),
        ]

    def quant_v2(w, jobs):
        steps = [qjob_steps(*j) for j in jobs]
        r = None
        for k in range(max(len(s) for s in steps)):
            for st in steps:
                if k < len(st):
                    r = st[k](w)
        return r

    PH = "abcdefg"
    def author_seq(s):
        w8("gpsimd", sV)
        w8("tensor", sG, CONST_G)
        w8("vector", sG, CONST_G)
        run("vector", lambda w: w.mset(_mk3(A_t, VAUG + 64, 256, 65, 1), 1.0))
        run("vector", lambda w: w.mset(_mk3(A_t, VSAUG + 64, 64, 65, 1), 1.0),
            inc=sV)
        ones_v = sV.n
        cast_done = [0] * NT
        tp_done = [0] * NT
        w8("sync", sV)
        for tc in range(NT):
            buf = XBF + (tc % 2) * 1024
            stg = ZSTG + (tc % 2) * 1024
            if tc >= 2:
                w8("sync", sA, cast_done[tc - 2])
            run("sync", lambda w, s=s, tc=tc, stg=stg:
                w.dma(Zp(stg, 1024), x_h[s, tc * 128:(tc + 1) * 128, :]),
                inc=cXB[tc % 2], k=16)
            w8("scalar", cXB[tc % 2], cXB[tc % 2].n)
            if tc >= 2:
                w8("scalar", sT, tp_done[tc - 2])
            run("scalar", lambda w, buf=buf, stg=stg:
                w.cp(Ab(buf, 1024), Zp(stg, 1024)), inc=sA)
            cast_done[tc] = sA.n
            w8("tensor", sA, cast_done[tc])
            for half in range(2):
                b = 4 + (2 * tc + half) % 4
                bank_wait(b)
                for j in range(4):
                    fb = half * 4 + j
                    run("tensor", lambda w, buf=buf, fb=fb, b=b, j=j:
                        w.mm(P[b][:, j * 128:(j + 1) * 128],
                             Ab(buf + fb * 128, 128), ident[:]),
                        inc=(sT if j == 3 else None))
                w8("scalar", sT)
                run("scalar", lambda w, tc=tc, half=half, b=b:
                    w.cp(_mk3(A_t, XT + half * 4 * 2048 + tc * 128, 4, 2048, 128),
                         _mk3(P[b], 0, 4, 128, 128)),
                    inc=sA)
                bank_set(b, sA)
            tp_done[tc] = sT.n
        xT_done_v = sA.n
        if s == 0 and "xT" in dbg:
            w8("sync", sA, xT_done_v)
            run("sync", lambda w: w.dma(dbg["xT"], Ab(XT, 16384)), inc=cDBG, k=16)
            w8("sync", cDBG)
            w8("vector", cDBG)
            w8("tensor", cDBG)

        if PH.index(upto) < PH.index("b"):
            return
        def load_w(wi):
            w8("sync", sT)
            w8("sync", cPS)
            run("sync", lambda w, wi=wi:
                w.dma(_mk3(A_t, WQKV, 8, 1024, 1024),
                      wat_h[wi].rearrange("(c p) n -> p c n", p=128)),
                inc=cWQ, k=16)
            return cWQ.n

        wch = (cXB0, cXB1, cX0, cX1, cZU0, cZU1, cSTG, cZ)

        def load_w_chunks(wi):
            w8("sync", sT)
            w8("sync", cPS)
            lds = []
            for dqc in range(8):
                run("sync", lambda w, wi=wi, dqc=dqc:
                    w.dma(_mk3(A_t, WQKV, 8, 1024, 128, lo=dqc * 128),
                          wat_h[wi][:, dqc * 128:(dqc + 1) * 128]
                          .rearrange("(c p) n -> p c n", p=128)),
                    inc=wch[dqc], k=16)
                lds.append(wch[dqc].n)
            return lds

        for wi, (wh, dst, scale) in enumerate(
                ((0, QT, True), (1, KT, False))):
            lds = load_w_chunks(wh)
            w8("tensor", sA, xT_done_v)
            for dqc in range(8):
                b0 = 4 * (dqc % 2)
                w8("tensor", wch[dqc], lds[dqc])
                for qt in range(NQT):
                    bank_wait(b0 + qt)
                for fc in range(8):
                    for qt in range(NQT):
                        run("tensor", lambda w, fc=fc, dqc=dqc, qt=qt, b0=b0:
                            w.mm(P[b0 + qt][:],
                                 Ab(WQKV + fc * 1024 + dqc * 128, 128),
                                 Ab(XT + fc * 2048 + qt * 512, 512),
                                 start=(fc == 0), stop=(fc == 7)),
                            inc=(sT if (fc == 7 and qt == NQT - 1) else None))
                w8("scalar", sT)
                for qt in range(NQT):
                    if scale:
                        run("scalar", lambda w, dst=dst, dqc=dqc, qt=qt, b0=b0:
                            w.act(Ab(dst + dqc * 2048 + qt * 512, 512),
                                  P[b0 + qt][:], AF.Copy, scale=sig8), inc=sA)
                    else:
                        run("scalar", lambda w, dst=dst, dqc=dqc, qt=qt, b0=b0:
                            w.cp(Ab(dst + dqc * 2048 + qt * 512, 512),
                                 P[b0 + qt][:]), inc=sA)
                    bank_set(b0 + qt, sA)
                p1_tick(1)
        qk_done_v = sA.n
        w8("vector", sA, qk_done_v)
        for hp in range(8):
            run("vector", lambda w, hp=hp:
                w.cp(Ab(KST + hp * 512, 512),
                     A_t[:, KT + hp * 2048:KT + (hp + 1) * 2048:4]), inc=sV)
        kst_done_v = sV.n
        ldv = load_w(2)
        w8("tensor", cWQ, ldv)
        for tc in range(NT):
            vb0 = 4 * (tc % 2)
            for b in (0, 1):
                bank_wait(vb0 + b)
            for fc in range(8):
                for b in (0, 1):
                    run("tensor", lambda w, tc=tc, fc=fc, b=b, vb0=vb0:
                        w.mm(P[vb0 + b][:],
                             Ab(XT + fc * 2048 + tc * 128, 128),
                             Ab(WQKV + fc * 1024 + b * 512, 512),
                             start=(fc == 0), stop=(fc == 7)),
                        inc=(sT if (fc == 7 and b == 1) else None))
            w8("scalar", sT)
            for b in (0, 1):
                run("scalar", lambda w, tc=tc, b=b, vb0=vb0:
                    w.cp(_mk3(A_t, VAUG + tc * 1040 + b * 8 * 65, 8, 65, 64),
                         _mk3(P[vb0 + b], 0, 8, 64, 64)), inc=sA)
                bank_set(vb0 + b, sA)
        for c in range(4):
            sb0 = 2 if c % 2 == 0 else 6
            for bi in (0, 1):
                bank_wait(sb0 + bi)
            for fc in range(8):
                for bi in (0, 1):
                    run("tensor", lambda w, c=c, fc=fc, bi=bi, sb0=sb0:
                        w.mm(P[sb0 + bi][:],
                             A_t[:, XT + fc * 2048 + 512 * c:
                                 XT + fc * 2048 + 512 * (c + 1):4],
                             Ab(WQKV + fc * 1024 + bi * 512, 512),
                             start=(fc == 0), stop=(fc == 7)),
                        inc=(sT if (fc == 7 and bi == 1) else None))
            w8("scalar", sT)
            for bi in (0, 1):
                run("scalar", lambda w, c=c, bi=bi, sb0=sb0:
                    w.cp(_mk3(A_t, VSAUG + c * 1040 + bi * 8 * 65, 8, 65, 64),
                         _mk3(P[sb0 + bi], 0, 8, 64, 64)), inc=sA)
                bank_set(sb0 + bi, sA)
        qkv_done_v = sA.n
        qkv_done_sem = sA
        qkv_done_t = sT.n
        if s == 0:
            for key, off, wdt in (("qT", QT, 16384), ("kT", KT, 16384),
                                  ("vaug", VAUG, 16640), ("kst", KST, 4096),
                                  ("vsaug", VSAUG, 4160)):
                if key in dbg:
                    w8("sync", sV, qkv_done_v)
                    run("sync", lambda w, key=key, off=off, wdt=wdt:
                        w.dma(dbg[key], Ab(off, wdt)), inc=cDBG, k=16)
                    w8("sync", cDBG)
                    w8("vector", cDBG)
                    w8("tensor", cDBG)

        if PH.index(upto) < PH.index("c"):
            return
        zpack_dmas = []
        stg_dma = [0]
        w8("tensor", sV, max(ones_v, kst_done_v))
        w8("tensor", sA, qkv_done_v)
        lm = [0] * (NH + 1)

        def local_tile(h, j):
            hp, hr = h // 2, h % 2
            r0 = 64 * hr
            eb = EXPL + (h % 2) * EXPL_STRIDE
            qlo = 128 * j - 8
            ust = 8 if j == 0 else 0
            uend = 136 if j == NT - 1 else 144
            wdt = uend - ust
            b = 4 + (j % 2)
            bank_wait(b)
            run("tensor", lambda w, hp=hp, r0=r0, j=j, qlo=qlo, ust=ust,
                wdt=wdt, b=b:
                w.mm(P[b][:, 0:wdt],
                     A_t[r0:r0 + 64, KT + hp * 2048 + 128 * j:
                         KT + hp * 2048 + 128 * (j + 1)],
                     A_t[r0:r0 + 64, QT + hp * 2048 + qlo + ust:
                         QT + hp * 2048 + qlo + ust + wdt]),
                inc=sT)
            w8("scalar", sT)
            run("scalar", lambda w, j=j, ust=ust, wdt=wdt, b=b, eb=eb:
                w.act(Ab(eb + j * 144 + ust, wdt), P[b][:, 0:wdt], AF.Exp),
                inc=sA)
            bank_set(b, sA)
            w8("vector", sA)
            run("vector", lambda w, j=j, ust=ust, wdt=wdt, eb=eb:
                w.tt(Ab(eb + j * 144 + ust, wdt),
                     Ab(eb + j * 144 + ust, wdt),
                     maskl[:, ust:ust + wdt], op=ALU.mult), inc=sV)

        for j in range(NT):
            local_tile(0, j)
        lm[0] = sV.n

        for h in range(NH):
            hp, hr = h // 2, h % 2
            r0 = 64 * hr
            eb = EXPL + (h % 2) * EXPL_STRIDE
            for qt in range(NQT):
                sb = (0, 1, 6, 7)
                ob = 2 + (qt % 2)
                bank_wait(ob)
                exp_a = [0] * 4
                for c in range(4):
                    b = sb[c]
                    bank_wait(b)
                    run("tensor", lambda w, hp=hp, r0=r0, c=c, qt=qt, b=b:
                        w.mm(P[b][:],
                             Ab(KST + hp * 512 + 128 * c, 128)
                             [r0:r0 + 64, :],
                             A_t[r0:r0 + 64, QT + hp * 2048 + qt * 512:
                                 QT + hp * 2048 + qt * 512 + 512]),
                        inc=sT)
                    w8("scalar", sT)
                    run("scalar", lambda w, c=c, b=b:
                        w.act(Ab(EXPS + c * 512, 512), P[b][:], AF.Exp), inc=sA)
                    exp_a[c] = sA.n
                    bank_set(b, sA)
                if h + 1 < NH:
                    for j in range(4 * qt, 4 * qt + 4):
                        local_tile(h + 1, j)
                for c in range(4):
                    w8("tensor", sA, exp_a[c])
                    run("tensor", lambda w, c=c, h=h, ob=ob:
                        w.mm(P[ob][0:65, :],
                             Ab(VSAUG + (c * 16 + h) * 65, 65),
                             Ab(EXPS + c * 512, 512),
                             start=(c == 0), stop=False),
                        inc=sT)
                w8("tensor", sV, lm[h])
                q0, q1 = 512 * qt, 512 * qt + 512
                hits = []
                for j in range(max(0, 4 * qt - 1), min(NT, 4 * qt + 5)):
                    qlo = 128 * j - 8
                    ust = 8 if j == 0 else 0
                    uend = 136 if j == NT - 1 else 144
                    a = max(qlo + ust, q0)
                    bnd = min(qlo + uend, q1)
                    if a < bnd:
                        hits.append((j, qlo, a, bnd))
                for idx, (j, qlo, a, bnd) in enumerate(hits):
                    last = idx == len(hits) - 1
                    run("tensor", lambda w, j=j, qlo=qlo, a=a, bnd=bnd, h=h,
                        q0=q0, last=last, eb=eb, ob=ob:
                        w.mm(P[ob][0:65, a - q0:bnd - q0],
                             Ab(VAUG + (j * 16 + h) * 65, 65),
                             Ab(eb + j * 144 + (a - qlo), bnd - a),
                             start=False, stop=last),
                        inc=(sT if last else None))
                pv_t = sT.n
                w8("vector", sT, pv_t)
                if hr == 0:
                    run("vector", lambda w, hp=hp, qt=qt, ob=ob:
                        w.cp(A_t[0:64, XT + hp * 2048 + qt * 512:
                                 XT + hp * 2048 + qt * 512 + 512],
                             P[ob][0:64, :]), inc=sV)
                else:
                    w8("vector", cSTG, stg_dma[0])
                    run("vector", lambda w, qt=qt, ob=ob:
                        w.cp(A_t[0:65, STG:STG + 512], P[ob][0:65, :]), inc=sV)
                    stg_v = sV.n
                    w8("sync", sV, stg_v)
                    run("sync", lambda w, hp=hp, qt=qt:
                        w.dma(A_t[64:128, XT + hp * 2048 + qt * 512:
                                  XT + hp * 2048 + qt * 512 + 512],
                              A_t[0:64, STG:STG + 512]), inc=cSTG, k=16)
                    stg_dma[0] = cSTG.n
                src_z = (P[ob][64:65, :]) if hr == 0 else (A_t[64:65, STG:STG + 512])
                run("vector", lambda w, qt=qt, src_z=src_z:
                    w.cp(Z_t[64:65, ZSTG + qt * 512:ZSTG + qt * 512 + 512],
                         src_z), inc=sV)
                bank_set(ob, sV)
                if qt % 2 == 1:
                    p1_tick(1)
            if h + 1 < NH:
                lm[h + 1] = sV.n
            w8("sync", sV)
            run("sync", lambda w, h=h:
                w.dma(Z_t[8 * h:8 * h + 8, ZZ:ZZ + 256],
                      Z_t[64:65, ZSTG:ZSTG + 2048]), inc=cZ, k=16)
            zpack_dmas.append(cZ.n)
            w8("vector", cZ, zpack_dmas[-1])
        att_unnorm_v = sV.n
        att_evac_a = sA.n

        if PH.index(upto) < PH.index("d"):
            return
        w8("vector", cZ, zpack_dmas[-1])
        w8("vector", cSTG)
        run("vector", lambda w: w.recip(Zp(RZP, 256), Zp(ZZ, 256)), inc=sV)
        rzp_v = sV.n
        w8("vector", sA, att_evac_a)
        if s == 0 and "zz" in dbg:
            w8("sync", sV)
            run("sync", lambda w: w.dma(dbg["zz"], Zp(ZZ, 256)), inc=cDBG, k=16)
            w8("sync", cDBG)
        rz_mm_t = [0, 0, 0]
        czu3 = (cZU0, cZU1, cSTG)

        def rz_unpack(h):
            s3 = h % 3
            w8("sync", sV, rzp_v)
            w8("sync", sT, rz_mm_t[s3])
            run("sync", lambda w, h=h, s3=s3:
                w.dma(Z_t[32 * s3:32 * s3 + 1, RZR:RZR + 2048],
                      Z_t[8 * h:8 * h + 8, RZP:RZP + 256]), inc=czu3[s3], k=16)
            return czu3[s3].n

        up_done = [0] * (NH + 2)
        up_done[0] = rz_unpack(0)
        up_done[1] = rz_unpack(1)
        for hp in range(8):
            for b in range(4, 8):
                bank_wait(b)
            for hr in range(2):
                h = 2 * hp + hr
                if h + 2 < NH:
                    up_done[h + 2] = rz_unpack(h + 2)
                s3 = h % 3
                w8("tensor", czu3[s3], up_done[h])
                for qt in range(NQT):
                    run("tensor", lambda w, qt=qt, hr=hr, s3=s3:
                        w.mm(P[4 + qt][64 * hr:64 * hr + 64, :],
                             ones2f[32 * s3:32 * s3 + 1, 0:64],
                             Z_t[32 * s3:32 * s3 + 1,
                                 RZR + qt * 512:RZR + qt * 512 + 512]),
                        inc=(sT if qt == NQT - 1 else None))
                rz_mm_t[s3] = sT.n
            w8("vector", sT)
            for qt in range(NQT):
                run("vector", lambda w, hp=hp, qt=qt:
                    w.tt(Ab(XT + hp * 2048 + qt * 512, 512),
                         Ab(XT + hp * 2048 + qt * 512, 512),
                         P[4 + qt][:], op=ALU.mult), inc=sV)
                bank_set(4 + qt, sV)
            p1_tick(1)
        att_done_v = sV.n
        if s == 0 and "att" in dbg:
            w8("sync", sV)
            run("sync", lambda w: w.dma(dbg["att"], Ab(XT, 16384)), inc=cDBG, k=16)
            w8("sync", cDBG)
            w8("tensor", cDBG)
            w8("vector", cDBG)

        if PH.index(upto) < PH.index("e"):
            return
        ldo = load_w(3)
        w8("tensor", cWQ, ldo)
        w8("tensor", sV, att_done_v)
        for tc in range(NT):
            bsel = (0, 1) if tc % 2 == 0 else (2, 3)
            xe = ZSTG + (tc % 2) * 1024
            w8("sync", sV, att_done_v)
            w8("sync", cZ)
            w8("sync", cZU0)
            w8("sync", cZU1)
            run("sync", lambda w, s=s, tc=tc, xe=xe:
                w.dma(Zp(xe, 1024), x_h[s, tc * 128:(tc + 1) * 128, :]),
                inc=cX[tc % 2], k=16)
            xload = cX[tc % 2].n
            for b in bsel:
                bank_wait(b)
            for hp in range(8):
                for bi, b in enumerate(bsel):
                    run("tensor", lambda w, hp=hp, tc=tc, b=b, bi=bi:
                        w.mm(P[b][:],
                             Ab(XT + hp * 2048 + tc * 128, 128),
                             Ab(WQKV + hp * 1024 + bi * 512, 512),
                             start=(hp == 0), stop=False))
            for bi, b in enumerate(bsel):
                run("tensor", lambda w, b=b, bi=bi:
                    w.mm(P[b][:], ones1[:], obbf[:, bi * 512:bi * 512 + 512],
                         start=False, stop=True),
                    inc=(sT if bi == 1 else None))
            w8("vector", sT)
            w8("vector", cX[tc % 2], xload)
            w8("vector", cY1)
            w8("vector", cY)
            for bi, b in enumerate(bsel):
                run("vector", lambda w, b=b, bi=bi, xe=xe:
                    w.tt(Zp(Y1P + bi * 512, 512), P[b][:],
                         Zp(xe + bi * 512, 512), op=ALU.add), inc=sV)
                bank_set(b, sV)
            ln_store(s, tc, y1_h, cY1)
            p1_tick(1)
        y1_done_g[s] = cY1.n
        return

    def author_wload():
        while p1_idx[0] < len(p1_jobs):
            p1_tick(1)
        w8("sync", sV)
        w8("sync", sT)
        w8("sync", cWS, p1_stores)
        for ic in range(22):
            run("sync", lambda w, ic=ic:
                w.dmat(Ab(WDT + ic * 1024, 1024), wdq_h[:, ic * 128:(ic + 1) * 128]),
                inc=cWT, k=16)
        for fc in range(1, 8):
            run("sync", lambda w, fc=fc:
                w.dmat(Ab(WUT + fc * 2816, 2816), wuq_h[:, fc * 128:(fc + 1) * 128]),
                inc=cWT, k=16)
        for fc in range(8):
            run("sync", lambda w, fc=fc:
                w.dmat(Ab(WGT + fc * 2816, 2816), wgq_h[:, fc * 128:(fc + 1) * 128]),
                inc=cWT, k=16)
        run("sync", lambda w:
            w.dmat(Ab(WUT, 2816), wuq_h[:, 0:128]), inc=cWT, k=16)
        wts_loaded_g[0] = cWT.n

    def author_ffn(s):
        y1_done = y1_done_g[s]
        w8("tensor", cWT, wts_loaded_g[0])
        y1qt_v = [0] * (NT + 1)
        y1q_rd = [0, 0]
        hq_v = [0] * (NT + 1)
        down_t = [0]
        hq_rd = [0, 0]
        xe_ld = [None] * (NT + 1)

        y1_sab = [0] * (NT + 1)

        def stageQ_load(tc):
            p = tc % 2
            xe = (ZSTG, ZSTG + 1024, WRAW)[tc % 3]
            w8("sync", sV)
            w8("sync", cY1, y1_done)
            run("sync", lambda w, s=s, tc=tc, xe=xe:
                w.dma(Zp(xe, 1024), y1_h[s, tc * 128:(tc + 1) * 128, :]),
                inc=cX[p], k=16)
            xe_ld[tc] = (tc % 3, p, cX[p].n)

        def y1_signabs(tc, gate_v):
            b3, p, ld = xe_ld[tc]
            xe = (ZSTG, ZSTG + 1024, WRAW)[b3]
            w8("vector", cX[p], ld)
            run("vector", lambda w, xe=xe:
                w.red(Zp(QS + 88, 64).unsqueeze(2), blk3(Zp(xe, 1024)),
                      op=ALU.max, absval=True), inc=sV)
            w8("scalar", cX[p], ld)
            w8("scalar", sV, gate_v)
            run("scalar", lambda w, xe=xe:
                w.act(Ab(SGNY, 1024), Zp(xe, 1024), AF.Sign))
            run("scalar", lambda w, xe=xe:
                w.act(Zp(HHY, 1024), Zp(xe, 1024), AF.Abs), inc=sA)
            y1_sab[tc] = sA.n

        def y1_job(tc):
            p = tc % 2
            return (Zp(HHY, 1024), Ab(Y1Q + p * 1024, 1024), 1024,
                    Ab(CY1B, 1024), Ab(SGNY, 1024),
                    Zp(QS + 88, 64), Zp(QS + 264, 64), Ab(VBY, 1024), True)

        def stageQ_a(tc):
            stageQ_load(tc)
            y1_signabs(tc, sV.n)
            w8("vector", sA, y1_sab[tc])
            w8("vector", sT, y1q_rd[tc % 2])
            run("vector", lambda w, tc=tc:
                quant_v2(w, [y1_job(tc)]), inc=sV)
            return sV.n

        def stageQ_b(tc, q_v):
            p = tc % 2
            w8("tensor", sV, q_v)
            for half in range(2):
                b = 6 + half
                bank_wait(b)
                for j in range(4):
                    fb = half * 4 + j
                    run("tensor", lambda w, fb=fb, b=b, j=j, p=p:
                        w.mm(P[b][:, j * 128:(j + 1) * 128],
                             Ab(Y1Q + p * 1024 + fb * 128, 128), ident[:]),
                        inc=(sT if j == 3 else None))
                w8("scalar", sT)
                run("scalar", lambda w, half=half, b=b, p=p:
                    w.cp(Ab(Y1QT + p * 1024 + half * 512, 512), P[b][:]), inc=sA)
                bank_set(b, sA)
            y1q_rd[p] = sT.n
            y1qt_v[tc] = sA.n

        qv0 = stageQ_a(0)
        stageQ_b(0, qv0)

        def M1(tc, ytc=None):
            p = tc % 2
            jq = [0]
            PIECES = ((0, 512), (512, 512), (1024, 384))

            def sign_abs_slice(poff, pw, gate_v):
                w8("scalar", sV, gate_v)
                run("scalar", lambda w, poff=poff, pw=pw:
                    w.act(Ab(SGN + poff, pw), Zp(HH + poff, pw), AF.Sign))
                run("scalar", lambda w, poff=poff, pw=pw:
                    w.act(Zp(HH + poff, pw), Zp(HH + poff, pw), AF.Abs), inc=sA)

            mult_v = [0, 0, 0]
            mult_g = [0] * 6
            hq0_v = [0]
            for half in range(2):
                hoff = half * 1408
                if half == 1 and ytc is not None:
                    y1_signabs(ytc, mult_v[2])
                for pi, (poff, pw) in enumerate(PIECES):
                    po = hoff + poff
                    bg = 0 if pi % 2 == 0 else 2
                    bu = 1 if pi % 2 == 0 else 3
                    for b in (bg, bu):
                        bank_wait(b)
                    w8("tensor", sA, y1qt_v[tc])
                    for fc in range(8):
                        run("tensor", lambda w, fc=fc, po=po, pw=pw, bg=bg, p=p:
                            w.mm(P[bg][:, 0:pw], Ab(Y1QT + p * 1024 + fc * 128, 128),
                                 Ab(WGT + fc * 2816 + po, pw),
                                 start=(fc == 0), stop=(fc == 7)))
                        run("tensor", lambda w, fc=fc, po=po, pw=pw, bu=bu, p=p:
                            w.mm(P[bu][:, 0:pw], Ab(Y1QT + p * 1024 + fc * 128, 128),
                                 Ab(WUT + fc * 2816 + po, pw),
                                 start=(fc == 0), stop=(fc == 7)),
                            inc=(sT if fc == 7 else None))
                    gpi = half * 3 + pi
                    sgp = (SGP, SGP2, SGP3)[gpi % 3]
                    w8("scalar", sT)
                    if gpi >= 3:
                        w8("scalar", sV, mult_g[gpi - 3])
                    run("scalar", lambda w, pw=pw, bg=bg, sgp=sgp:
                        w.act(Ab(sgp, pw), P[bg][:, 0:pw], AF.Silu), inc=sA)
                    silu_a = sA.n
                    bank_set(bg, sA)
                    if pi >= 1:
                        pv_off, pv_w = PIECES[pi - 1]
                        gate = mult_v[pi - 1]
                        if half == 1:
                            gate = max(gate, hq0_v[0])
                        sign_abs_slice(pv_off, pv_w, gate)
                    w8("vector", sA, silu_a)
                    run("vector", lambda w, poff=poff, pw=pw, bu=bu, sgp=sgp:
                        w.tt(Zp(HH + poff, pw), Ab(sgp, pw), P[bu][:, 0:pw],
                             op=ALU.mult), inc=sV)
                    run("vector", lambda w, poff=poff, pw=pw:
                        w.red(Zp(QS + poff // 16, pw // 16).unsqueeze(2),
                              blk3(Zp(HH + poff, pw)), op=ALU.max,
                              absval=True), inc=sV)
                    mult_v[pi] = sV.n
                    mult_g[gpi] = sV.n
                    bank_set(bu, sV)
                gate = mult_v[2]
                if half == 1:
                    gate = max(gate, hq0_v[0])
                sign_abs_slice(PIECES[2][0], PIECES[2][1], gate)
                w8("vector", sA)
                w8("vector", sT, hq_rd[p])
                hjob = (Zp(HH, 1408), Ab(HQ + p * 2816 + hoff, 1408), 1408,
                        Ab(CCH, 1408), Ab(SGN, 1408),
                        Zp(QS, 88), Zp(QS + 176, 88), Ab(VB2, 1408), True)
                if half == 1 and ytc is not None:
                    w8("vector", sT, y1q_rd[ytc % 2])
                    run("vector", lambda w, hjob=hjob, ytc=ytc:
                        quant_v2(w, [hjob, y1_job(ytc)]), inc=sV)
                    jq[0] = sV.n
                else:
                    run("vector", lambda w, hjob=hjob:
                        quant_v2(w, [hjob]), inc=sV)
                    hq0_v[0] = sV.n
            hq_v[tc] = sV.n
            return jq[0]

        def M2a(tc):
            p = tc % 2
            w8("tensor", sV, hq_v[tc])
            done = 0
            bsel_t = 0
            while done < 22:
                cnt = min(4, 22 - done)
                b = 6 + (bsel_t % 2)
                bsel_t += 1
                bank_wait(b)
                for j in range(cnt):
                    ic = done + j
                    run("tensor", lambda w, ic=ic, b=b, j=j, p=p:
                        w.mm(P[b][:, j * 128:(j + 1) * 128],
                             Ab(HQ + p * 2816 + ic * 128, 128), ident[:]),
                        inc=(sT if j == cnt - 1 else None))
                w8("scalar", sT)
                run("scalar", lambda w, done=done, cnt=cnt, b=b:
                    w.cp(Ab(HQT + done * 128, cnt * 128), P[b][:, 0:cnt * 128]),
                    inc=sA)
                bank_set(b, sA)
                done += cnt
            hq_rd[p] = sT.n
            hqt_v = sA.n
            for b in (4, 5):
                bank_wait(b)
            w8("tensor", sA, hqt_v)
            for ic in range(22):
                for bi, b in enumerate((4, 5)):
                    run("tensor", lambda w, ic=ic, b=b, bi=bi:
                        w.mm(P[b][:], Ab(HQT + ic * 128, 128),
                             Ab(WDT + ic * 1024 + bi * 512, 512),
                             start=(ic == 0), stop=(ic == 21)),
                        inc=(sT if (ic == 21 and bi == 1) else None))
            down_t[0] = sT.n

        def M2b(tc):
            b3, pe, ld = xe_ld[tc]
            xe = (ZSTG, ZSTG + 1024, WRAW)[b3]
            w8("vector", sT, down_t[0])
            w8("vector", cX[pe], ld)
            w8("vector", cY)
            w8("vector", cY1)
            for bi, b in enumerate((4, 5)):
                run("vector", lambda w, b=b, bi=bi, xe=xe:
                    w.tt(Zp(Y1P + bi * 512, 512), P[b][:],
                         Zp(xe + bi * 512, 512), op=ALU.add), inc=sV)
                bank_set(b, sV)
            ln_store(s, tc, y_h, cY, presilu=True)

        for tc in range(NT + 1):
            nxt = tc + 1 if tc + 1 <= NT - 1 else None
            if nxt is not None:
                stageQ_load(nxt)
            if tc <= NT - 1:
                qv = M1(tc, ytc=nxt)
            if tc >= 1:
                M2a(tc - 1)
            if nxt is not None:
                stageQ_b(nxt, qv)
            if tc >= 1:
                M2b(tc - 1)

    def ln_store(s, tc, dst_h, ch, presilu=False):
        run("vector", lambda w: w.bnstats(lns[:, 0:6], Zp(Y1P, 512)))
        run("vector", lambda w: w.bnstats(lns[:, 6:12], Zp(Y1P + 512, 512)))
        run("vector", lambda w: w.bnaggr(lns[:, 12:14], lns[:, 0:12]))
        run("vector", lambda w: w.ts(lns[:, 14:15], lns[:, 13:14], EPS, None,
                                     op0=ALU.add), inc=sV)
        w8("scalar", sV)
        run("scalar", lambda w: w.act(lns[:, 15:16], lns[:, 14:15], AF.Sqrt),
            inc=(None if presilu else sA))
        if presilu:
            run("scalar", lambda w:
                w.act(lns[0:1, 17:18], lns[0:1, 14:15], AF.Silu), inc=sA)
        w8("vector", sA)
        run("vector", lambda w: w.recip(lns[:, 16:17], lns[:, 15:16]))
        run("vector", lambda w:
            w.ts(Zp(Y1P, 1024), Zp(Y1P, 1024), lns[:, 12:13], lns[:, 16:17],
                 op0=ALU.subtract, op1=ALU.mult), inc=sV)
        w8("sync", sV)
        run("sync", lambda w, s=s, tc=tc, dst_h=dst_h:
            w.dma(dst_h[s, tc * 128:(tc + 1) * 128, :], Zp(Y1P, 1024)),
            inc=ch, k=16)

    p1_stores = P1_TOTAL if with_p1 else 0
    if not with_p1:
        p1_idx[0] = len(p1_jobs)
    y1_done_g = [0] * NSEQ
    wts_loaded_g = [0]
    for _s in range(seqs):
        p1_cap[0] = 34 if (_s == 0 and seqs > 1) else len(p1_jobs)
        author_seq(_s)
    if PH.index(upto) >= PH.index("f"):
        author_wload()
    if PH.index(upto) >= PH.index("g"):
        for _s in range(seqs):
            author_ffn(_s)

    for _s in (cOB, cWR, cWS, cWQ, cXB0, cXB1, cX0, cX1, cSTG, cZ, cWT,
               cY1, cY, cDBG, cPS, cZU0, cZU1, sV, sT, sA, sG):
        w8("sync", _s)

    with nc.Block() as block:
        @block.sync
        def _(eng):
            w = E(eng)
            for fn in rec["sync"]:
                fn(w)

        @block.tensor
        def _(eng):
            w = E(eng)
            for fn in rec["tensor"]:
                fn(w)

        @block.vector
        def _(eng):
            w = E(eng)
            for fn in rec["vector"]:
                fn(w)

        @block.scalar
        def _(eng):
            w = E(eng)
            for fn in rec["scalar"]:
                fn(w)

        @block.gpsimd
        def _(eng):
            w = E(eng)
            for fn in rec["gpsimd"]:
                fn(w)

    es.close()
    return nc


def kernel(L0_attn_x, L0_attn_q_w, L0_attn_k_w, L0_attn_v_w, L0_attn_o_w,
           L0_attn_o_b, L0_attn_unity_scale, L0_ffn_w_gate, L0_ffn_w_up,
           L0_ffn_w_down):
    x = np.ascontiguousarray(np.asarray(L0_attn_x, dtype=np.float32))
    args = {
        "qw": np.ascontiguousarray(np.asarray(L0_attn_q_w, np.float32)),
        "kw": np.ascontiguousarray(np.asarray(L0_attn_k_w, np.float32)),
        "vw": np.ascontiguousarray(np.asarray(L0_attn_v_w, np.float32)),
        "ow": np.ascontiguousarray(np.asarray(L0_attn_o_w, np.float32)),
        "ob": np.ascontiguousarray(np.asarray(L0_attn_o_b, np.float32)),
        "us": np.asarray(L0_attn_unity_scale, np.float32).reshape(1),
        "wg": np.ascontiguousarray(np.asarray(L0_ffn_w_gate, np.float32)),
        "wu": np.ascontiguousarray(np.asarray(L0_ffn_w_up, np.float32)),
        "wd": np.ascontiguousarray(np.asarray(L0_ffn_w_down, np.float32)),
    }
    nc = bass.Bass("TRN2", target_bir_lowering=False)
    build(nc)
    in_maps = [dict(args, x=np.ascontiguousarray(x[2 * c:2 * c + 2]))
               for c in range(8)]
    res = run_bass_kernel_spmd(nc, in_maps, core_ids=list(range(8)))
    return np.concatenate([r["y"] for r in res.results], axis=0)


# version 39
# speedup vs baseline: 1.0133x; 1.0133x over previous
import numpy as np
import concourse.bass as bass
import concourse.mybir as mybir
from concourse.bass_utils import run_bass_kernel_spmd
from contextlib import ExitStack

F32 = mybir.dt.float32
BF16 = mybir.dt.bfloat16
AF = mybir.ActivationFunctionType
ALU = mybir.AluOpType
AX = mybir.AxisListType

NSEQ, S, DM, NH, DH, NHP, DI = 2, 2048, 1024, 16, 64, 8, 2816
NT, NQT, BLK, EPS = 16, 4, 16, 1e-5

XT = 0
QT = 16384
KT = 32768
VAUG = 49152
KST = 65792
VSAUG = 69888
XBF = 74048
EXPS = 74048
EXPL = 80448
EXPL_STRIDE = 2304
P1T = 77376
WQKV = 80448
STG = 88640
A_COLS = 89152
WGT = 0
WUT = 22528
WDT = 45056
Y1Q = 67584
Y1QT = 69632
SGN = 71680
CCH = 73088
SGP = 74496
HQ = 75008
HQT = 80640
SGNY = 83456
VB2 = 84480
CY1B = 85888
VBY = 86912
SGP2 = 87936
SGP3 = 88640

ZSTG = 0
ZZ = 2048
RZP = 2304
RZR = 0
Y1P = 2560
HH = 3584
QS = 4992
WRAW = 5344
P1S = 6368
P1A = HH
HHY = 6496
Z_COLS = 7520


class Sem:
    def __init__(self, h):
        self.h = h
        self.n = 0


class E:
    def __init__(self, eng):
        self.e = eng
        self.wr = []
        self.rd = []
        self.waits = {}

    def wait(self, sem, val):
        if val <= 0 or self.waits.get(id(sem), -1) >= val:
            return
        self.e.wait_ge(sem.h, val)
        self.waits[id(sem)] = val

    def drain(self):
        self.e.drain()
        self.wr = []
        self.rd = []

    @staticmethod
    def _rng(a):
        off = a.offset - a.base_partition() * a.ap[0][0]
        span = 1
        for st, n in a.ap[1:]:
            span = max(span, abs(st) * (n - 1) + 1)
        es = 2 if a.dtype == BF16 else 4
        return (a.tensor.name, off * es, (off + span) * es)

    @staticmethod
    def _hit(r, s):
        return r[0] == s[0] and r[1] < s[2] and s[1] < r[2]

    def _pre(self, outs, ins):
        no = [self._rng(a) for a in outs]
        ni = [self._rng(a) for a in ins]
        conflict = any(self._hit(r, s) for r in ni for s in self.wr) or \
            any(self._hit(r, s) for r in no for s in self.wr) or \
            any(self._hit(r, s) for r in no for s in self.rd)
        if conflict:
            self.drain()
        self.wr.extend(no)
        self.rd.extend(ni)

    def cp(self, out, in_):
        self._pre([out], [in_])
        if hasattr(self.e, "tensor_copy"):
            return self.e.tensor_copy(out, in_)
        return self.e.copy(out, in_)

    def ts(self, out, in0, s1, s2=None, op0=ALU.mult, op1=ALU.bypass):
        ins = [in0] + [x for x in (s1, s2) if isinstance(x, bass.AP)]
        self._pre([out], ins)
        if s2 is None:
            return self.e.tensor_single_scalar(out, in0, s1, op=op0)
        return self.e.tensor_scalar(out, in0, s1, s2, op0=op0, op1=op1)

    def tt(self, out, a, b, op):
        self._pre([out], [a, b])
        return self.e.tensor_tensor(out, a, b, op=op)

    def stt(self, out, in0, scalar, in1, op0, op1):
        self._pre([out], [in0, in1])
        return self.e.scalar_tensor_tensor(out, in0, scalar, in1, op0=op0, op1=op1)

    def red(self, out, in_, op=ALU.max, absval=False):
        self._pre([out], [in_])
        return self.e.tensor_reduce(out, in_, axis=AX.X, op=op,
                                    apply_absolute_value=absval)

    def recip(self, out, in_):
        self._pre([out], [in_])
        return self.e.reciprocal(out, in_)

    def mset(self, ap, c):
        self._pre([ap], [])
        return self.e.memset(ap, c)

    def act(self, out, in_, f, bias=0.0, scale=1.0, accum=None):
        ins = [in_] + [x for x in (bias, scale) if isinstance(x, bass.AP)]
        self._pre([out] + ([accum] if accum is not None else []), ins)
        return self.e.activation(out, in_, f, bias=bias, scale=scale,
                                 accum_out=accum)

    def bnstats(self, out, in_):
        self._pre([out], [in_])
        return self.e.bn_stats(out, in_)

    def bnaggr(self, out, in_):
        self._pre([out], [in_])
        return self.e.bn_aggr(out, in_)

    def iota(self, out, pattern, base=0, chm=0):
        self._pre([out], [])
        return self.e.iota(out, pattern=pattern, base=base,
                           channel_multiplier=chm,
                           allow_small_or_imprecise_dtypes=True)

    def aff(self, out, in_, cmp, fill, base, pattern, chm):
        self._pre([out], [in_])
        return self.e.affine_select(out=out, in_=in_, compare_op=cmp, fill=fill,
                                    base=base, pattern=pattern,
                                    channel_multiplier=chm)

    def dma(self, out, in_):
        return self.e.dma_start(out=out, in_=in_)

    def dmat(self, out, in_):
        return self.e.dma_start_transpose(out, in_)

    def mm(self, out, lhsT, rhs, start=True, stop=True):
        return self.e.matmul(out, lhsT, rhs, start=start, stop=stop)

    def nop(self):
        return self.e.engine_nop()


def build(nc, dbg_keys=(), upto="g", seqs=NSEQ, with_p1=True):
    dbg = {}
    x_h = nc.dram_tensor("x", [NSEQ, S, DM], F32, kind="ExternalInput").ap()
    qw_h = nc.dram_tensor("qw", [DM, DM], F32, kind="ExternalInput").ap()
    kw_h = nc.dram_tensor("kw", [DM, DM], F32, kind="ExternalInput").ap()
    vw_h = nc.dram_tensor("vw", [DM, DM], F32, kind="ExternalInput").ap()
    ow_h = nc.dram_tensor("ow", [DM, DM], F32, kind="ExternalInput").ap()
    ob_h = nc.dram_tensor("ob", [DM], F32, kind="ExternalInput").ap()
    us_h = nc.dram_tensor("us", [1], F32, kind="ExternalInput").ap()
    wg_h = nc.dram_tensor("wg", [DI, DM], F32, kind="ExternalInput").ap()
    wu_h = nc.dram_tensor("wu", [DI, DM], F32, kind="ExternalInput").ap()
    wd_h = nc.dram_tensor("wd", [DM, DI], F32, kind="ExternalInput").ap()
    y_h = nc.dram_tensor("y", [NSEQ, S, DM], F32, kind="ExternalOutput").ap()
    wat_h = nc.dram_tensor("wattn_s", [4, DM, DM], BF16, kind="Internal").ap()
    wgq_h = nc.dram_tensor("wgq_s", [DI, DM], BF16, kind="Internal").ap()
    wuq_h = nc.dram_tensor("wuq_s", [DI, DM], BF16, kind="Internal").ap()
    wdq_h = nc.dram_tensor("wdq_s", [DM, DI], BF16, kind="Internal").ap()
    y1_h = nc.dram_tensor("y1_s", [NSEQ, S, DM], F32, kind="Internal").ap()
    for k in dbg_keys:
        shp = {"xT": [128, 16384], "qT": [128, 16384], "kT": [128, 16384],
               "vaug": [128, 16640], "kst": [128, 4096], "vsaug": [128, 4160],
               "att": [128, 16384], "zz": [128, 256], "expl": [128, 2304],
               "y1q": [128, 1024], "h0": [128, 1408], "hq": [128, 2816]}[k]
        dbg[k] = nc.dram_tensor("dbg_" + k, shp, BF16 if k not in ("zz", "h0")
                                else F32, kind="ExternalOutput").ap()

    es = ExitStack()
    A_t = es.enter_context(nc.sbuf_tensor("A", [128, A_COLS], BF16))
    Z_t = es.enter_context(nc.sbuf_tensor("Zf", [128, Z_COLS], F32))
    ident = es.enter_context(nc.sbuf_tensor("ident", [128, 128], BF16))
    maskl = es.enter_context(nc.sbuf_tensor("maskl", [128, 144], BF16))
    csml = es.enter_context(nc.sbuf_tensor("csml", [128, 4], F32))
    lns = es.enter_context(nc.sbuf_tensor("lns", [128, 20], F32))
    ones1 = es.enter_context(nc.sbuf_tensor("ones1", [1, 128], BF16))
    onesf = es.enter_context(nc.sbuf_tensor("onesf", [1, 128], F32))
    ones2f = es.enter_context(nc.sbuf_tensor("ones2f", [65, 128], F32))
    obbf = es.enter_context(nc.sbuf_tensor("obbf", [1, 1024], BF16))
    ussb = es.enter_context(nc.sbuf_tensor("ussb", [1, 1], F32))
    P = [es.enter_context(nc.psum_tensor("P%d" % i, [128, 512], F32))
         for i in range(8)]
    ch_names = ("sT", "sV", "sA", "sG", "cOB", "cWR", "cWS", "cWQ",
                "cXB0", "cXB1", "cX0", "cX1", "cSTG", "cZ", "cWT",
                "cY1", "cY", "cDBG", "cPS", "cZU0", "cZU1")
    sem_h = {k: es.enter_context(nc.semaphore(k)) for k in ch_names}
    (sT, sV, sA, sG, cOB, cWR, cWS, cWQ, cXB0, cXB1, cX0, cX1, cSTG, cZ,
     cWT, cY1, cY, cDBG, cPS, cZU0, cZU1) = (Sem(sem_h[k]) for k in ch_names)
    cZU = (cZU0, cZU1)
    cXB = (cXB0, cXB1)
    cX = (cX0, cX1)

    rec = {k: [] for k in ("sync", "tensor", "vector", "scalar", "gpsimd")}

    def w8(eng, sem, val=None):
        v = sem.n if val is None else val
        rec[eng].append(lambda w, s=sem, v=v: w.wait(s, v))

    def run(eng, fn, inc=None, k=1):
        if inc is not None:
            inc.n += k
        def thunk(w, fn=fn, inc=inc, k=k):
            ins = fn(w)
            if inc is not None:
                ins.then_inc(inc.h, k)
        rec[eng].append(thunk)

    bank_ev = {i: (sV, 0) for i in range(8)}

    def bank_wait(i, eng="tensor"):
        s, v = bank_ev[i]
        w8(eng, s, v)

    def bank_set(i, sem):
        bank_ev[i] = (sem, sem.n)

    Ab = lambda off, wdt: A_t[:, off:off + wdt]
    Zp = lambda off, wdt: Z_t[:, off:off + wdt]

    def blk3(ap):
        return ap.rearrange("p (nb blk) -> p nb blk", blk=BLK)

    def _mk3(t, base, n, step, wdt, lo=0):
        v = t[:, base:base + n * step]
        v = v.rearrange("p (c n) -> p c n", c=n)
        return v[:, :, lo:lo + wdt]

    run("vector", lambda w: w.mset(ident[:], 0.0))
    run("vector", lambda w: w.mset(maskl[:], 1.0))
    run("vector", lambda w: w.mset(ones1[:], 1.0))
    run("vector", lambda w: w.mset(onesf[:], 1.0))
    run("vector", lambda w: w.mset(ones2f[:], 1.0))
    run("vector", lambda w: w.mset(ident[:1, :1], 0.0), inc=sV)
    w8("gpsimd", sV)
    run("gpsimd", lambda w: w.aff(ident[:], ident[:], ALU.not_equal, 1.0, 0,
                                  [[-1, 128]], 1))
    run("gpsimd", lambda w: w.aff(maskl[:], maskl[:], ALU.is_ge, 0.0, 0,
                                  [[1, 144]], -1))
    run("gpsimd", lambda w: w.aff(maskl[:], maskl[:], ALU.is_ge, 0.0, 16,
                                  [[-1, 144]], 1))
    run("gpsimd", lambda w: w.iota(
        Z_t[0:1, ZZ:ZZ + 128].rearrange("o (g r) -> o g r", r=4),
        [[0, 32], [1, 4]], 0, 0))
    run("gpsimd", lambda w: w.ts(Z_t[0:1, ZZ:ZZ + 128], Z_t[0:1, ZZ:ZZ + 128],
                                 0.0, None, op0=ALU.is_gt), inc=sG)
    CONST_G0 = sG.n
    run("sync", lambda w: w.dma(Z_t[0:1, Y1P:Y1P + 1024], ob_h.unsqueeze(0)),
        inc=cOB, k=16)
    run("sync", lambda w: w.dma(ussb[:], us_h.unsqueeze(0)), inc=cOB, k=16)
    w8("vector", cOB)
    run("vector", lambda w: w.cp(obbf[:], Z_t[0:1, Y1P:Y1P + 1024]), inc=sV)
    w8("tensor", cOB)
    w8("tensor", sV)
    run("tensor", lambda w: w.mm(P[0][:, 0:1], onesf[:], ussb[:]), inc=sT)
    w8("scalar", sT)
    run("scalar", lambda w: w.act(csml[:, 0:1], P[0][:, 0:1], AF.Sigmoid), inc=sA)
    w8("vector", sA)
    run("vector", lambda w: w.ts(csml[:, 0:1], csml[:, 0:1], 0.125), inc=sV)
    w8("tensor", sG, CONST_G0)
    w8("tensor", sA)
    run("tensor", lambda w: w.mm(P[0][:, 2:3], Z_t[0:1, ZZ:ZZ + 128],
                                 onesf[0:1, 0:1]), inc=sT)
    w8("vector", sT)
    run("vector", lambda w: w.cp(csml[:, 1:2], P[0][:, 2:3]), inc=sV)
    w8("gpsimd", sV)
    run("gpsimd", lambda w: w.ts(maskl[:], maskl[:], csml[:, 1:2], op0=ALU.mult),
        inc=sG)
    CONST_G = sG.n
    bank_set(0, sV)
    sig8 = csml[:, 0:1]

    for wi, wh in enumerate((qw_h, kw_h, vw_h, ow_h)):
        for it in range(8):
            w8("gpsimd", sV)
            run("gpsimd", lambda w, wh=wh, it=it:
                w.dma(Zp(WRAW, 1024), wh[it * 128:(it + 1) * 128, :]),
                inc=cWR, k=16)
            w8("vector", cWR)
            w8("vector", cPS)
            run("vector", lambda w: w.cp(Ab(P1T, 1024), Zp(WRAW, 1024)), inc=sV)
            w8("gpsimd", sV)
            run("gpsimd", lambda w, wi=wi, it=it:
                w.dma(wat_h[wi, it * 128:(it + 1) * 128, :], Ab(P1T, 1024)),
                inc=cPS, k=16)

    p1_jobs = []
    for src_, dst_ in ((wg_h, wgq_h), (wu_h, wuq_h)):
        for it in range(DI // 128):
            p1_jobs.append((src_, dst_, it, 0, 1024))
    for it in range(DM // 128):
        for off in (0, 1024, 2048):
            p1_jobs.append((wd_h, wdq_h, it, off, min(1024, DI - off)))
    P1_TOTAL = 16 * len(p1_jobs)
    p1_idx = [0]

    p1_pref = [0]
    p1_sA = [0, 0]
    p1_sV = [0, 0]

    def p1_load(i):
        if i >= len(p1_jobs):
            return
        src_, dst_, it, off, pw = p1_jobs[i]
        w8("gpsimd", sA)
        run("gpsimd", lambda w, s=src_, it=it, o=off, pw=pw:
            w.dma(Zp(WRAW, pw), s[it * 128:(it + 1) * 128, o:o + pw]),
            inc=cWR, k=16)
        p1_pref[0] = i + 1

    p1_cap = [len(p1_jobs)]
    p1_phase = [0]

    def p1_tick(n=1):
        for _ in range(n):
            if p1_idx[0] >= min(p1_cap[0], len(p1_jobs)):
                return
            i = p1_idx[0]
            src_, dst_, it, off, pw = p1_jobs[i]
            if p1_phase[0] == 0:
                if p1_pref[0] <= i:
                    p1_load(i)
                w8("scalar", cWR)
                w8("scalar", sV)
                run("scalar", lambda w, pw=pw:
                    w.act(Ab(P1T + 2048, pw), Zp(WRAW, pw), AF.Sign))
                run("scalar", lambda w, pw=pw:
                    w.act(Zp(P1A, pw), Zp(WRAW, pw), AF.Abs), inc=sA)
                w8("vector", sA)
                w8("vector", cWS)
                w8("vector", cPS)
                run("vector", lambda w, pw=pw:
                    quant_v(w, Zp(P1A, pw), Ab(P1T, pw), pw, c_off=P1T + 1024,
                            sgn_ap=Ab(P1T + 2048, pw), qs_off=P1S, vb_off=-1,
                            part=0), inc=sV)
                p1_phase[0] = 1
            else:
                run("vector", lambda w, pw=pw:
                    quant_v(w, Zp(P1A, pw), Ab(P1T, pw), pw, c_off=P1T + 1024,
                            sgn_ap=Ab(P1T + 2048, pw), qs_off=P1S, vb_off=-1,
                            part=1), inc=sV)
                p1_sV[0] = sV.n
                p1_load(i + 1)
                w8("gpsimd", sV, p1_sV[0])
                run("gpsimd", lambda w, d=dst_, it=it, o=off, pw=pw:
                    w.dma(d[it * 128:(it + 1) * 128, o:o + pw], Ab(P1T, pw)),
                    inc=cWS, k=16)
                p1_phase[0] = 0
                p1_idx[0] += 1

    def quant_v(w, sf, dst_bf, pw, c_off=None, sgn_ap=None, qs_off=None,
                vb_off=None, part=None):
        nb = pw // BLK
        qs = QS if qs_off is None else qs_off
        nb_stride = 176 if qs_off is None else 64
        bmax = Zp(qs, nb)
        sv = Zp(qs + nb_stride, nb)
        c = Ab(CCH if c_off is None else c_off, pw)
        sgn = Ab(SGN, pw) if sgn_ap is None else sgn_ap
        vb = (sf if vb_off == -1
              else Ab(VB2 if vb_off is None else vb_off, pw))
        if part != 1:
            w.red(bmax.unsqueeze(2), blk3(sf), op=ALU.max, absval=True)
            w.ts(sv, bmax, 1.0 / 6.0, 1e-12, op0=ALU.mult, op1=ALU.max)
            w.tt(blk3(sgn), blk3(sgn),
                 sv.unsqueeze(2).broadcast_to([128, nb, BLK]),
                 op=ALU.mult)
            w.recip(sv, sv)
            sbc = sv.unsqueeze(2).broadcast_to([128, nb, BLK])
            r = w.tt(blk3(vb), blk3(sf), sbc, op=ALU.mult)
            if part == 0:
                return r
        w.ts(c, vb, 5.0, 2.0, op0=ALU.is_gt, op1=ALU.mult)
        w.ts(dst_bf, vb, 0.25, None, op0=ALU.is_gt)
        w.stt(c, vb, 2.5, c, op0=ALU.is_gt, op1=ALU.add)
        w.stt(dst_bf, vb, 0.75, dst_bf, op0=ALU.is_gt, op1=ALU.add)
        w.stt(c, vb, 3.5, c, op0=ALU.is_gt, op1=ALU.add)
        w.stt(dst_bf, vb, 1.25, dst_bf, op0=ALU.is_gt, op1=ALU.add)
        w.stt(dst_bf, vb, 1.75, dst_bf, op0=ALU.is_gt, op1=ALU.add)
        w.stt(dst_bf, dst_bf, 0.5, c, op0=ALU.mult, op1=ALU.add)
        return w.tt(dst_bf, dst_bf, sgn, op=ALU.mult)

    def qjob_steps(sf, dst_bf, pw, c, sgn, bmax, sv, vb, pre_red=False):
        nb = pw // BLK
        sbc = sv.unsqueeze(2).broadcast_to([128, nb, BLK])
        return ([] if pre_red else [
            lambda w: w.red(bmax.unsqueeze(2), blk3(sf), op=ALU.max,
                            absval=True)]) + [
            lambda w: w.ts(sv, bmax, 1.0 / 6.0, 1e-12, op0=ALU.mult,
                           op1=ALU.max),
            lambda w: w.tt(blk3(sgn), blk3(sgn), sbc, op=ALU.mult),
            lambda w: w.recip(sv, sv),
            lambda w: w.tt(blk3(vb), blk3(sf), sbc, op=ALU.mult),
            lambda w: w.ts(c, vb, 5.0, 2.0, op0=ALU.is_gt, op1=ALU.mult),
            lambda w: w.ts(dst_bf, vb, 0.25, None, op0=ALU.is_gt),
            lambda w: w.stt(c, vb, 2.5, c, op0=ALU.is_gt, op1=ALU.add),
            lambda w: w.stt(dst_bf, vb, 0.75, dst_bf, op0=ALU.is_gt,
                            op1=ALU.add),
            lambda w: w.stt(c, vb, 3.5, c, op0=ALU.is_gt, op1=ALU.add),
            lambda w: w.stt(dst_bf, vb, 1.25, dst_bf, op0=ALU.is_gt,
                            op1=ALU.add),
            lambda w: w.stt(dst_bf, vb, 1.75, dst_bf, op0=ALU.is_gt,
                            op1=ALU.add),
            lambda w: w.stt(dst_bf, dst_bf, 0.5, c, op0=ALU.mult, op1=ALU.add),
            lambda w: w.tt(dst_bf, dst_bf, sgn, op=ALU.mult),
        ]

    def quant_v2(w, jobs):
        steps = [qjob_steps(*j) for j in jobs]
        r = None
        for k in range(max(len(s) for s in steps)):
            for st in steps:
                if k < len(st):
                    r = st[k](w)
        return r

    PH = "abcdefg"
    def author_seq(s):
        w8("gpsimd", sV)
        w8("tensor", sG, CONST_G)
        w8("vector", sG, CONST_G)
        run("vector", lambda w: w.mset(_mk3(A_t, VAUG + 64, 256, 65, 1), 1.0))
        run("vector", lambda w: w.mset(_mk3(A_t, VSAUG + 64, 64, 65, 1), 1.0),
            inc=sV)
        ones_v = sV.n
        cast_done = [0] * NT
        tp_done = [0] * NT
        w8("sync", sV)
        for tc in range(NT):
            buf = XBF + (tc % 2) * 1024
            stg = ZSTG + (tc % 2) * 1024
            if tc >= 2:
                w8("sync", sA, cast_done[tc - 2])
            run("sync", lambda w, s=s, tc=tc, stg=stg:
                w.dma(Zp(stg, 1024), x_h[s, tc * 128:(tc + 1) * 128, :]),
                inc=cXB[tc % 2], k=16)
            w8("scalar", cXB[tc % 2], cXB[tc % 2].n)
            if tc >= 2:
                w8("scalar", sT, tp_done[tc - 2])
            run("scalar", lambda w, buf=buf, stg=stg:
                w.cp(Ab(buf, 1024), Zp(stg, 1024)), inc=sA)
            cast_done[tc] = sA.n
            w8("tensor", sA, cast_done[tc])
            for half in range(2):
                b = 4 + (2 * tc + half) % 4
                bank_wait(b)
                for j in range(4):
                    fb = half * 4 + j
                    run("tensor", lambda w, buf=buf, fb=fb, b=b, j=j:
                        w.mm(P[b][:, j * 128:(j + 1) * 128],
                             Ab(buf + fb * 128, 128), ident[:]),
                        inc=(sT if j == 3 else None))
                w8("scalar", sT)
                run("scalar", lambda w, tc=tc, half=half, b=b:
                    w.cp(_mk3(A_t, XT + half * 4 * 2048 + tc * 128, 4, 2048, 128),
                         _mk3(P[b], 0, 4, 128, 128)),
                    inc=sA)
                bank_set(b, sA)
            tp_done[tc] = sT.n
        xT_done_v = sA.n
        if s == 0 and "xT" in dbg:
            w8("sync", sA, xT_done_v)
            run("sync", lambda w: w.dma(dbg["xT"], Ab(XT, 16384)), inc=cDBG, k=16)
            w8("sync", cDBG)
            w8("vector", cDBG)
            w8("tensor", cDBG)

        if PH.index(upto) < PH.index("b"):
            return
        def load_w(wi):
            w8("sync", sT)
            w8("sync", cPS)
            run("sync", lambda w, wi=wi:
                w.dma(_mk3(A_t, WQKV, 8, 1024, 1024),
                      wat_h[wi].rearrange("(c p) n -> p c n", p=128)),
                inc=cWQ, k=16)
            return cWQ.n

        wch = (cXB0, cXB1, cX0, cX1, cZU0, cZU1, cSTG, cZ)

        def load_w_chunks(wi):
            w8("sync", sT)
            w8("sync", cPS)
            lds = []
            for dqc in range(8):
                run("sync", lambda w, wi=wi, dqc=dqc:
                    w.dma(_mk3(A_t, WQKV, 8, 1024, 128, lo=dqc * 128),
                          wat_h[wi][:, dqc * 128:(dqc + 1) * 128]
                          .rearrange("(c p) n -> p c n", p=128)),
                    inc=wch[dqc], k=16)
                lds.append(wch[dqc].n)
            return lds

        for wi, (wh, dst, scale) in enumerate(
                ((0, QT, True), (1, KT, False))):
            lds = load_w_chunks(wh)
            w8("tensor", sA, xT_done_v)
            for dqc in range(8):
                b0 = 4 * (dqc % 2)
                w8("tensor", wch[dqc], lds[dqc])
                for qt in range(NQT):
                    bank_wait(b0 + qt)
                for fc in range(8):
                    for qt in range(NQT):
                        run("tensor", lambda w, fc=fc, dqc=dqc, qt=qt, b0=b0:
                            w.mm(P[b0 + qt][:],
                                 Ab(WQKV + fc * 1024 + dqc * 128, 128),
                                 Ab(XT + fc * 2048 + qt * 512, 512),
                                 start=(fc == 0), stop=(fc == 7)),
                            inc=(sT if (fc == 7 and qt == NQT - 1) else None))
                w8("scalar", sT)
                for qt in range(NQT):
                    if scale:
                        run("scalar", lambda w, dst=dst, dqc=dqc, qt=qt, b0=b0:
                            w.act(Ab(dst + dqc * 2048 + qt * 512, 512),
                                  P[b0 + qt][:], AF.Copy, scale=sig8), inc=sA)
                    else:
                        run("scalar", lambda w, dst=dst, dqc=dqc, qt=qt, b0=b0:
                            w.cp(Ab(dst + dqc * 2048 + qt * 512, 512),
                                 P[b0 + qt][:]), inc=sA)
                    bank_set(b0 + qt, sA)
                p1_tick(1)
        qk_done_v = sA.n
        w8("vector", sA, qk_done_v)
        for hp in range(8):
            run("vector", lambda w, hp=hp:
                w.cp(Ab(KST + hp * 512, 512),
                     A_t[:, KT + hp * 2048:KT + (hp + 1) * 2048:4]), inc=sV)
        kst_done_v = sV.n
        ldv = load_w(2)
        w8("tensor", cWQ, ldv)
        for tc in range(NT):
            vb0 = 4 * (tc % 2)
            for b in (0, 1):
                bank_wait(vb0 + b)
            for fc in range(8):
                for b in (0, 1):
                    run("tensor", lambda w, tc=tc, fc=fc, b=b, vb0=vb0:
                        w.mm(P[vb0 + b][:],
                             Ab(XT + fc * 2048 + tc * 128, 128),
                             Ab(WQKV + fc * 1024 + b * 512, 512),
                             start=(fc == 0), stop=(fc == 7)),
                        inc=(sT if (fc == 7 and b == 1) else None))
            w8("scalar", sT)
            for b in (0, 1):
                run("scalar", lambda w, tc=tc, b=b, vb0=vb0:
                    w.cp(_mk3(A_t, VAUG + tc * 1040 + b * 8 * 65, 8, 65, 64),
                         _mk3(P[vb0 + b], 0, 8, 64, 64)), inc=sA)
                bank_set(vb0 + b, sA)
        for c in range(4):
            sb0 = 2 if c % 2 == 0 else 6
            for bi in (0, 1):
                bank_wait(sb0 + bi)
            for fc in range(8):
                for bi in (0, 1):
                    run("tensor", lambda w, c=c, fc=fc, bi=bi, sb0=sb0:
                        w.mm(P[sb0 + bi][:],
                             A_t[:, XT + fc * 2048 + 512 * c:
                                 XT + fc * 2048 + 512 * (c + 1):4],
                             Ab(WQKV + fc * 1024 + bi * 512, 512),
                             start=(fc == 0), stop=(fc == 7)),
                        inc=(sT if (fc == 7 and bi == 1) else None))
            w8("scalar", sT)
            for bi in (0, 1):
                run("scalar", lambda w, c=c, bi=bi, sb0=sb0:
                    w.cp(_mk3(A_t, VSAUG + c * 1040 + bi * 8 * 65, 8, 65, 64),
                         _mk3(P[sb0 + bi], 0, 8, 64, 64)), inc=sA)
                bank_set(sb0 + bi, sA)
        qkv_done_v = sA.n
        qkv_done_sem = sA
        qkv_done_t = sT.n
        if s == 0:
            for key, off, wdt in (("qT", QT, 16384), ("kT", KT, 16384),
                                  ("vaug", VAUG, 16640), ("kst", KST, 4096),
                                  ("vsaug", VSAUG, 4160)):
                if key in dbg:
                    w8("sync", sV, qkv_done_v)
                    run("sync", lambda w, key=key, off=off, wdt=wdt:
                        w.dma(dbg[key], Ab(off, wdt)), inc=cDBG, k=16)
                    w8("sync", cDBG)
                    w8("vector", cDBG)
                    w8("tensor", cDBG)

        if PH.index(upto) < PH.index("c"):
            return
        zpack_dmas = []
        stg_dma = [0]
        w8("tensor", sV, max(ones_v, kst_done_v))
        w8("tensor", sA, qkv_done_v)
        lm = [0] * (NH + 1)
        lmask_n = [0] * (NH + 1)

        def local_tile(h, j):
            hp, hr = h // 2, h % 2
            r0 = 64 * hr
            eb = EXPL + (h % 2) * EXPL_STRIDE
            qlo = 128 * j - 8
            ust = 8 if j == 0 else 0
            uend = 136 if j == NT - 1 else 144
            wdt = uend - ust
            b = 4 + (j % 2)
            bank_wait(b)
            run("tensor", lambda w, hp=hp, r0=r0, j=j, qlo=qlo, ust=ust,
                wdt=wdt, b=b:
                w.mm(P[b][:, 0:wdt],
                     A_t[r0:r0 + 64, KT + hp * 2048 + 128 * j:
                         KT + hp * 2048 + 128 * (j + 1)],
                     A_t[r0:r0 + 64, QT + hp * 2048 + qlo + ust:
                         QT + hp * 2048 + qlo + ust + wdt]),
                inc=sT)
            w8("scalar", sT)
            run("scalar", lambda w, j=j, ust=ust, wdt=wdt, b=b, eb=eb:
                w.act(Ab(eb + j * 144 + ust, wdt), P[b][:, 0:wdt], AF.Exp),
                inc=sA)
            bank_set(b, sA)
            w8("vector", sA)
            run("vector", lambda w, j=j, ust=ust, wdt=wdt, eb=eb:
                w.tt(Ab(eb + j * 144 + ust, wdt),
                     Ab(eb + j * 144 + ust, wdt),
                     maskl[:, ust:ust + wdt], op=ALU.mult), inc=sV)
            lmask_n[h] = sV.n

        for j in range(NT):
            local_tile(0, j)
        lm[0] = sV.n

        for h in range(NH):
            hp, hr = h // 2, h % 2
            r0 = 64 * hr
            eb = EXPL + (h % 2) * EXPL_STRIDE
            for qt in range(NQT):
                sb = (0, 1, 6, 7)
                ob = 2 + (qt % 2)
                bank_wait(ob)
                exp_a = [0] * 4
                for c in range(4):
                    b = sb[c]
                    bank_wait(b)
                    run("tensor", lambda w, hp=hp, r0=r0, c=c, qt=qt, b=b:
                        w.mm(P[b][:],
                             Ab(KST + hp * 512 + 128 * c, 128)
                             [r0:r0 + 64, :],
                             A_t[r0:r0 + 64, QT + hp * 2048 + qt * 512:
                                 QT + hp * 2048 + qt * 512 + 512]),
                        inc=sT)
                    w8("scalar", sT)
                    run("scalar", lambda w, c=c, b=b:
                        w.act(Ab(EXPS + c * 512, 512), P[b][:], AF.Exp), inc=sA)
                    exp_a[c] = sA.n
                    bank_set(b, sA)
                if h + 1 < NH:
                    for j in range(4 * qt, 4 * qt + 4):
                        local_tile(h + 1, j)
                for c in range(4):
                    w8("tensor", sA, exp_a[c])
                    run("tensor", lambda w, c=c, h=h, ob=ob:
                        w.mm(P[ob][0:65, :],
                             Ab(VSAUG + (c * 16 + h) * 65, 65),
                             Ab(EXPS + c * 512, 512),
                             start=(c == 0), stop=False),
                        inc=sT)
                w8("tensor", sV, lm[h])
                q0, q1 = 512 * qt, 512 * qt + 512
                hits = []
                for j in range(max(0, 4 * qt - 1), min(NT, 4 * qt + 5)):
                    qlo = 128 * j - 8
                    ust = 8 if j == 0 else 0
                    uend = 136 if j == NT - 1 else 144
                    a = max(qlo + ust, q0)
                    bnd = min(qlo + uend, q1)
                    if a < bnd:
                        hits.append((j, qlo, a, bnd))
                for idx, (j, qlo, a, bnd) in enumerate(hits):
                    last = idx == len(hits) - 1
                    run("tensor", lambda w, j=j, qlo=qlo, a=a, bnd=bnd, h=h,
                        q0=q0, last=last, eb=eb, ob=ob:
                        w.mm(P[ob][0:65, a - q0:bnd - q0],
                             Ab(VAUG + (j * 16 + h) * 65, 65),
                             Ab(eb + j * 144 + (a - qlo), bnd - a),
                             start=False, stop=last),
                        inc=(sT if last else None))
                pv_t = sT.n
                w8("vector", sT, pv_t)
                if hr == 0:
                    run("vector", lambda w, hp=hp, qt=qt, ob=ob:
                        w.cp(A_t[0:64, XT + hp * 2048 + qt * 512:
                                 XT + hp * 2048 + qt * 512 + 512],
                             P[ob][0:64, :]), inc=sV)
                else:
                    w8("vector", cSTG, stg_dma[0])
                    run("vector", lambda w, qt=qt, ob=ob:
                        w.cp(A_t[0:65, STG:STG + 512], P[ob][0:65, :]), inc=sV)
                    stg_v = sV.n
                    w8("sync", sV, stg_v)
                    run("sync", lambda w, hp=hp, qt=qt:
                        w.dma(A_t[64:128, XT + hp * 2048 + qt * 512:
                                  XT + hp * 2048 + qt * 512 + 512],
                              A_t[0:64, STG:STG + 512]), inc=cSTG, k=16)
                    stg_dma[0] = cSTG.n
                src_z = (P[ob][64:65, :]) if hr == 0 else (A_t[64:65, STG:STG + 512])
                run("vector", lambda w, qt=qt, src_z=src_z:
                    w.cp(Z_t[64:65, ZSTG + qt * 512:ZSTG + qt * 512 + 512],
                         src_z), inc=sV)
                bank_set(ob, sV)
                if qt % 2 == 1:
                    p1_tick(1)
            if h + 1 < NH:
                lm[h + 1] = lmask_n[h + 1]
            w8("sync", sV)
            run("sync", lambda w, h=h:
                w.dma(Z_t[8 * h:8 * h + 8, ZZ:ZZ + 256],
                      Z_t[64:65, ZSTG:ZSTG + 2048]), inc=cZ, k=16)
            zpack_dmas.append(cZ.n)
            w8("vector", cZ, zpack_dmas[-1])
        att_unnorm_v = sV.n
        att_evac_a = sA.n

        if PH.index(upto) < PH.index("d"):
            return
        w8("vector", cZ, zpack_dmas[-1])
        w8("vector", cSTG)
        run("vector", lambda w: w.recip(Zp(RZP, 256), Zp(ZZ, 256)), inc=sV)
        rzp_v = sV.n
        w8("vector", sA, att_evac_a)
        if s == 0 and "zz" in dbg:
            w8("sync", sV)
            run("sync", lambda w: w.dma(dbg["zz"], Zp(ZZ, 256)), inc=cDBG, k=16)
            w8("sync", cDBG)
        rz_mm_t = [0, 0, 0]
        czu3 = (cZU0, cZU1, cSTG)

        def rz_unpack(h):
            s3 = h % 3
            w8("sync", sV, rzp_v)
            w8("sync", sT, rz_mm_t[s3])
            run("sync", lambda w, h=h, s3=s3:
                w.dma(Z_t[32 * s3:32 * s3 + 1, RZR:RZR + 2048],
                      Z_t[8 * h:8 * h + 8, RZP:RZP + 256]), inc=czu3[s3], k=16)
            return czu3[s3].n

        up_done = [0] * (NH + 2)
        up_done[0] = rz_unpack(0)
        up_done[1] = rz_unpack(1)
        for hp in range(8):
            for b in range(4, 8):
                bank_wait(b)
            for hr in range(2):
                h = 2 * hp + hr
                if h + 2 < NH:
                    up_done[h + 2] = rz_unpack(h + 2)
                s3 = h % 3
                w8("tensor", czu3[s3], up_done[h])
                for qt in range(NQT):
                    run("tensor", lambda w, qt=qt, hr=hr, s3=s3:
                        w.mm(P[4 + qt][64 * hr:64 * hr + 64, :],
                             ones2f[32 * s3:32 * s3 + 1, 0:64],
                             Z_t[32 * s3:32 * s3 + 1,
                                 RZR + qt * 512:RZR + qt * 512 + 512]),
                        inc=(sT if qt == NQT - 1 else None))
                rz_mm_t[s3] = sT.n
            w8("vector", sT)
            for qt in range(NQT):
                run("vector", lambda w, hp=hp, qt=qt:
                    w.tt(Ab(XT + hp * 2048 + qt * 512, 512),
                         Ab(XT + hp * 2048 + qt * 512, 512),
                         P[4 + qt][:], op=ALU.mult), inc=sV)
                bank_set(4 + qt, sV)
            p1_tick(1)
        att_done_v = sV.n
        if s == 0 and "att" in dbg:
            w8("sync", sV)
            run("sync", lambda w: w.dma(dbg["att"], Ab(XT, 16384)), inc=cDBG, k=16)
            w8("sync", cDBG)
            w8("tensor", cDBG)
            w8("vector", cDBG)

        if PH.index(upto) < PH.index("e"):
            return
        ldo = load_w(3)
        w8("tensor", cWQ, ldo)
        w8("tensor", sV, att_done_v)
        for tc in range(NT):
            bsel = (0, 1) if tc % 2 == 0 else (2, 3)
            xe = ZSTG + (tc % 2) * 1024
            w8("sync", sV, att_done_v)
            w8("sync", cZ)
            w8("sync", cZU0)
            w8("sync", cZU1)
            run("sync", lambda w, s=s, tc=tc, xe=xe:
                w.dma(Zp(xe, 1024), x_h[s, tc * 128:(tc + 1) * 128, :]),
                inc=cX[tc % 2], k=16)
            xload = cX[tc % 2].n
            for b in bsel:
                bank_wait(b)
            for hp in range(8):
                for bi, b in enumerate(bsel):
                    run("tensor", lambda w, hp=hp, tc=tc, b=b, bi=bi:
                        w.mm(P[b][:],
                             Ab(XT + hp * 2048 + tc * 128, 128),
                             Ab(WQKV + hp * 1024 + bi * 512, 512),
                             start=(hp == 0), stop=False))
            for bi, b in enumerate(bsel):
                run("tensor", lambda w, b=b, bi=bi:
                    w.mm(P[b][:], ones1[:], obbf[:, bi * 512:bi * 512 + 512],
                         start=False, stop=True),
                    inc=(sT if bi == 1 else None))
            w8("vector", sT)
            w8("vector", cX[tc % 2], xload)
            w8("vector", cY1)
            w8("vector", cY)
            for bi, b in enumerate(bsel):
                run("vector", lambda w, b=b, bi=bi, xe=xe:
                    w.tt(Zp(Y1P + bi * 512, 512), P[b][:],
                         Zp(xe + bi * 512, 512), op=ALU.add), inc=sV)
                bank_set(b, sV)
            ln_store(s, tc, y1_h, cY1)
            p1_tick(1)
        y1_done_g[s] = cY1.n
        return

    def author_wload():
        while p1_idx[0] < len(p1_jobs):
            p1_tick(1)
        w8("sync", sV)
        w8("sync", sT)
        w8("sync", cWS, p1_stores)
        for ic in range(22):
            run("sync", lambda w, ic=ic:
                w.dmat(Ab(WDT + ic * 1024, 1024), wdq_h[:, ic * 128:(ic + 1) * 128]),
                inc=cWT, k=16)
        for fc in range(1, 8):
            run("sync", lambda w, fc=fc:
                w.dmat(Ab(WUT + fc * 2816, 2816), wuq_h[:, fc * 128:(fc + 1) * 128]),
                inc=cWT, k=16)
        for fc in range(8):
            run("sync", lambda w, fc=fc:
                w.dmat(Ab(WGT + fc * 2816, 2816), wgq_h[:, fc * 128:(fc + 1) * 128]),
                inc=cWT, k=16)
        run("sync", lambda w:
            w.dmat(Ab(WUT, 2816), wuq_h[:, 0:128]), inc=cWT, k=16)
        wts_loaded_g[0] = cWT.n

    def author_ffn(s):
        y1_done = y1_done_g[s]
        w8("tensor", cWT, wts_loaded_g[0])
        y1qt_v = [0] * (NT + 1)
        y1q_rd = [0, 0]
        hq_v = [0] * (NT + 1)
        hq_rd = [0, 0]
        xe_ld = [None] * (NT + 1)

        y1_sab = [0] * (NT + 1)

        def stageQ_load(tc):
            p = tc % 2
            xe = (ZSTG, ZSTG + 1024, WRAW)[tc % 3]
            w8("sync", sV)
            w8("sync", cY1, y1_done)
            run("sync", lambda w, s=s, tc=tc, xe=xe:
                w.dma(Zp(xe, 1024), y1_h[s, tc * 128:(tc + 1) * 128, :]),
                inc=cX[p], k=16)
            xe_ld[tc] = (tc % 3, p, cX[p].n)

        def y1_signabs(tc, gate_v):
            b3, p, ld = xe_ld[tc]
            xe = (ZSTG, ZSTG + 1024, WRAW)[b3]
            w8("vector", cX[p], ld)
            run("vector", lambda w, xe=xe:
                w.red(Zp(QS + 88, 64).unsqueeze(2), blk3(Zp(xe, 1024)),
                      op=ALU.max, absval=True), inc=sV)
            w8("scalar", cX[p], ld)
            w8("scalar", sV, gate_v)
            run("scalar", lambda w, xe=xe:
                w.act(Ab(SGNY, 1024), Zp(xe, 1024), AF.Sign))
            run("scalar", lambda w, xe=xe:
                w.act(Zp(HHY, 1024), Zp(xe, 1024), AF.Abs), inc=sA)
            y1_sab[tc] = sA.n

        def y1_job(tc):
            p = tc % 2
            return (Zp(HHY, 1024), Ab(Y1Q + p * 1024, 1024), 1024,
                    Ab(CY1B, 1024), Ab(SGNY, 1024),
                    Zp(QS + 88, 64), Zp(QS + 264, 64), Ab(VBY, 1024), True)

        def stageQ_a(tc):
            stageQ_load(tc)
            y1_signabs(tc, sV.n)
            w8("vector", sA, y1_sab[tc])
            w8("vector", sT, y1q_rd[tc % 2])
            run("vector", lambda w, tc=tc:
                quant_v2(w, [y1_job(tc)]), inc=sV)
            return sV.n

        def stageQ_b(tc, q_v):
            p = tc % 2
            w8("tensor", sV, q_v)
            for half in range(2):
                b = 6 + half
                bank_wait(b)
                for j in range(4):
                    fb = half * 4 + j
                    run("tensor", lambda w, fb=fb, b=b, j=j, p=p:
                        w.mm(P[b][:, j * 128:(j + 1) * 128],
                             Ab(Y1Q + p * 1024 + fb * 128, 128), ident[:]),
                        inc=(sT if j == 3 else None))
                w8("scalar", sT)
                run("scalar", lambda w, half=half, b=b, p=p:
                    w.cp(Ab(Y1QT + p * 1024 + half * 512, 512), P[b][:]), inc=sA)
                bank_set(b, sA)
            y1q_rd[p] = sT.n
            y1qt_v[tc] = sA.n

        qv0 = stageQ_a(0)
        stageQ_b(0, qv0)

        def M1(tc, ytc=None):
            p = tc % 2
            jq = [0]
            PIECES = ((0, 512), (512, 512), (1024, 384))

            def sign_abs_slice(poff, pw, gate_v):
                w8("scalar", sV, gate_v)
                run("scalar", lambda w, poff=poff, pw=pw:
                    w.act(Ab(SGN + poff, pw), Zp(HH + poff, pw), AF.Sign))
                run("scalar", lambda w, poff=poff, pw=pw:
                    w.act(Zp(HH + poff, pw), Zp(HH + poff, pw), AF.Abs), inc=sA)

            mult_v = [0, 0, 0]
            mult_g = [0] * 6
            hq0_v = [0]
            for half in range(2):
                hoff = half * 1408
                if half == 1 and ytc is not None:
                    y1_signabs(ytc, mult_v[2])
                for pi, (poff, pw) in enumerate(PIECES):
                    po = hoff + poff
                    bg = 0 if pi % 2 == 0 else 2
                    bu = 1 if pi % 2 == 0 else 3
                    for b in (bg, bu):
                        bank_wait(b)
                    w8("tensor", sA, y1qt_v[tc])
                    for fc in range(8):
                        run("tensor", lambda w, fc=fc, po=po, pw=pw, bg=bg, p=p:
                            w.mm(P[bg][:, 0:pw], Ab(Y1QT + p * 1024 + fc * 128, 128),
                                 Ab(WGT + fc * 2816 + po, pw),
                                 start=(fc == 0), stop=(fc == 7)))
                        run("tensor", lambda w, fc=fc, po=po, pw=pw, bu=bu, p=p:
                            w.mm(P[bu][:, 0:pw], Ab(Y1QT + p * 1024 + fc * 128, 128),
                                 Ab(WUT + fc * 2816 + po, pw),
                                 start=(fc == 0), stop=(fc == 7)),
                            inc=(sT if fc == 7 else None))
                    gpi = half * 3 + pi
                    sgp = (SGP, SGP2, SGP3)[gpi % 3]
                    w8("scalar", sT)
                    if gpi >= 3:
                        w8("scalar", sV, mult_g[gpi - 3])
                    run("scalar", lambda w, pw=pw, bg=bg, sgp=sgp:
                        w.act(Ab(sgp, pw), P[bg][:, 0:pw], AF.Silu), inc=sA)
                    silu_a = sA.n
                    bank_set(bg, sA)
                    if pi >= 1:
                        pv_off, pv_w = PIECES[pi - 1]
                        gate = mult_v[pi - 1]
                        if half == 1:
                            gate = max(gate, hq0_v[0])
                        sign_abs_slice(pv_off, pv_w, gate)
                    w8("vector", sA, silu_a)
                    run("vector", lambda w, poff=poff, pw=pw, bu=bu, sgp=sgp:
                        w.tt(Zp(HH + poff, pw), Ab(sgp, pw), P[bu][:, 0:pw],
                             op=ALU.mult), inc=sV)
                    run("vector", lambda w, poff=poff, pw=pw:
                        w.red(Zp(QS + poff // 16, pw // 16).unsqueeze(2),
                              blk3(Zp(HH + poff, pw)), op=ALU.max,
                              absval=True), inc=sV)
                    mult_v[pi] = sV.n
                    mult_g[gpi] = sV.n
                    bank_set(bu, sV)
                gate = mult_v[2]
                if half == 1:
                    gate = max(gate, hq0_v[0])
                sign_abs_slice(PIECES[2][0], PIECES[2][1], gate)
                w8("vector", sA)
                w8("vector", sT, hq_rd[p])
                hjob = (Zp(HH, 1408), Ab(HQ + p * 2816 + hoff, 1408), 1408,
                        Ab(CCH, 1408), Ab(SGN, 1408),
                        Zp(QS, 88), Zp(QS + 176, 88), Ab(VB2, 1408), True)
                if half == 1 and ytc is not None:
                    w8("vector", sT, y1q_rd[ytc % 2])
                    run("vector", lambda w, hjob=hjob, ytc=ytc:
                        quant_v2(w, [hjob, y1_job(ytc)]), inc=sV)
                    jq[0] = sV.n
                else:
                    run("vector", lambda w, hjob=hjob:
                        quant_v2(w, [hjob]), inc=sV)
                    hq0_v[0] = sV.n
            hq_v[tc] = sV.n
            return jq[0]

        def M2a(tc):
            p = tc % 2
            w8("tensor", sV, hq_v[tc])
            done = 0
            bsel_t = 0
            while done < 22:
                cnt = min(4, 22 - done)
                b = 6 + (bsel_t % 2)
                bsel_t += 1
                bank_wait(b)
                for j in range(cnt):
                    ic = done + j
                    run("tensor", lambda w, ic=ic, b=b, j=j, p=p:
                        w.mm(P[b][:, j * 128:(j + 1) * 128],
                             Ab(HQ + p * 2816 + ic * 128, 128), ident[:]),
                        inc=(sT if j == cnt - 1 else None))
                w8("scalar", sT)
                run("scalar", lambda w, done=done, cnt=cnt, b=b:
                    w.cp(Ab(HQT + done * 128, cnt * 128), P[b][:, 0:cnt * 128]),
                    inc=sA)
                bank_set(b, sA)
                done += cnt
            hq_rd[p] = sT.n
            hqt_v = sA.n
            for b in (4, 5):
                bank_wait(b)
            w8("tensor", sA, hqt_v)
            for ic in range(22):
                for bi, b in enumerate((4, 5)):
                    run("tensor", lambda w, ic=ic, b=b, bi=bi:
                        w.mm(P[b][:], Ab(HQT + ic * 128, 128),
                             Ab(WDT + ic * 1024 + bi * 512, 512),
                             start=(ic == 0), stop=(ic == 21)),
                        inc=(sT if (ic == 21 and bi == 1) else None))

        def M2b(tc):
            b3, pe, ld = xe_ld[tc]
            xe = (ZSTG, ZSTG + 1024, WRAW)[b3]
            w8("vector", sT)
            w8("vector", cX[pe], ld)
            w8("vector", cY)
            w8("vector", cY1)
            for bi, b in enumerate((4, 5)):
                run("vector", lambda w, b=b, bi=bi, xe=xe:
                    w.tt(Zp(Y1P + bi * 512, 512), P[b][:],
                         Zp(xe + bi * 512, 512), op=ALU.add), inc=sV)
                bank_set(b, sV)
            ln_store(s, tc, y_h, cY, presilu=True)

        for tc in range(NT + 1):
            nxt = tc + 1 if tc + 1 <= NT - 1 else None
            if nxt is not None:
                stageQ_load(nxt)
            if tc <= NT - 1:
                qv = M1(tc, ytc=nxt)
            if tc >= 1:
                M2a(tc - 1)
            if nxt is not None:
                stageQ_b(nxt, qv)
            if tc >= 1:
                M2b(tc - 1)

    def ln_store(s, tc, dst_h, ch, presilu=False):
        run("vector", lambda w: w.bnstats(lns[:, 0:6], Zp(Y1P, 512)))
        run("vector", lambda w: w.bnstats(lns[:, 6:12], Zp(Y1P + 512, 512)))
        run("vector", lambda w: w.bnaggr(lns[:, 12:14], lns[:, 0:12]))
        run("vector", lambda w: w.ts(lns[:, 14:15], lns[:, 13:14], EPS, None,
                                     op0=ALU.add), inc=sV)
        w8("scalar", sV)
        run("scalar", lambda w: w.act(lns[:, 15:16], lns[:, 14:15], AF.Sqrt),
            inc=(None if presilu else sA))
        if presilu:
            run("scalar", lambda w:
                w.act(lns[0:1, 17:18], lns[0:1, 14:15], AF.Silu), inc=sA)
        w8("vector", sA)
        run("vector", lambda w: w.recip(lns[:, 16:17], lns[:, 15:16]))
        run("vector", lambda w:
            w.ts(Zp(Y1P, 1024), Zp(Y1P, 1024), lns[:, 12:13], lns[:, 16:17],
                 op0=ALU.subtract, op1=ALU.mult), inc=sV)
        w8("sync", sV)
        run("sync", lambda w, s=s, tc=tc, dst_h=dst_h:
            w.dma(dst_h[s, tc * 128:(tc + 1) * 128, :], Zp(Y1P, 1024)),
            inc=ch, k=16)

    p1_stores = P1_TOTAL if with_p1 else 0
    if not with_p1:
        p1_idx[0] = len(p1_jobs)
    y1_done_g = [0] * NSEQ
    wts_loaded_g = [0]
    for _s in range(seqs):
        p1_cap[0] = 34 if (_s == 0 and seqs > 1) else len(p1_jobs)
        author_seq(_s)
    if PH.index(upto) >= PH.index("f"):
        author_wload()
    if PH.index(upto) >= PH.index("g"):
        for _s in range(seqs):
            author_ffn(_s)

    for _s in (cOB, cWR, cWS, cWQ, cXB0, cXB1, cX0, cX1, cSTG, cZ, cWT,
               cY1, cY, cDBG, cPS, cZU0, cZU1, sV, sT, sA, sG):
        w8("sync", _s)

    with nc.Block() as block:
        @block.sync
        def _(eng):
            w = E(eng)
            for fn in rec["sync"]:
                fn(w)

        @block.tensor
        def _(eng):
            w = E(eng)
            for fn in rec["tensor"]:
                fn(w)

        @block.vector
        def _(eng):
            w = E(eng)
            for fn in rec["vector"]:
                fn(w)

        @block.scalar
        def _(eng):
            w = E(eng)
            for fn in rec["scalar"]:
                fn(w)

        @block.gpsimd
        def _(eng):
            w = E(eng)
            for fn in rec["gpsimd"]:
                fn(w)

    es.close()
    return nc


def kernel(L0_attn_x, L0_attn_q_w, L0_attn_k_w, L0_attn_v_w, L0_attn_o_w,
           L0_attn_o_b, L0_attn_unity_scale, L0_ffn_w_gate, L0_ffn_w_up,
           L0_ffn_w_down):
    x = np.ascontiguousarray(np.asarray(L0_attn_x, dtype=np.float32))
    args = {
        "qw": np.ascontiguousarray(np.asarray(L0_attn_q_w, np.float32)),
        "kw": np.ascontiguousarray(np.asarray(L0_attn_k_w, np.float32)),
        "vw": np.ascontiguousarray(np.asarray(L0_attn_v_w, np.float32)),
        "ow": np.ascontiguousarray(np.asarray(L0_attn_o_w, np.float32)),
        "ob": np.ascontiguousarray(np.asarray(L0_attn_o_b, np.float32)),
        "us": np.asarray(L0_attn_unity_scale, np.float32).reshape(1),
        "wg": np.ascontiguousarray(np.asarray(L0_ffn_w_gate, np.float32)),
        "wu": np.ascontiguousarray(np.asarray(L0_ffn_w_up, np.float32)),
        "wd": np.ascontiguousarray(np.asarray(L0_ffn_w_down, np.float32)),
    }
    nc = bass.Bass("TRN2", target_bir_lowering=False)
    build(nc)
    in_maps = [dict(args, x=np.ascontiguousarray(x[2 * c:2 * c + 2]))
               for c in range(8)]
    res = run_bass_kernel_spmd(nc, in_maps, core_ids=list(range(8)))
    return np.concatenate([r["y"] for r in res.results], axis=0)


# version 40 (final)
# speedup vs baseline: 1.0256x; 1.0121x over previous
import numpy as np
import concourse.bass as bass
import concourse.mybir as mybir
from concourse.bass_utils import run_bass_kernel_spmd
from contextlib import ExitStack

F32 = mybir.dt.float32
BF16 = mybir.dt.bfloat16
AF = mybir.ActivationFunctionType
ALU = mybir.AluOpType
AX = mybir.AxisListType

NSEQ, S, DM, NH, DH, NHP, DI = 2, 2048, 1024, 16, 64, 8, 2816
NT, NQT, BLK, EPS = 16, 4, 16, 1e-5

XT = 0
QT = 16384
KT = 32768
VAUG = 49152
KST = 65792
VSAUG = 69888
XBF = 74048
EXPS = 74048
EXPL = 80448
EXPL_STRIDE = 2304
P1T = 77376
WQKV = 80448
STG = 88640
A_COLS = 89152
WGT = 0
WUT = 22528
WDT = 45056
Y1Q = 67584
Y1QT = 69632
SGN = 71680
CCH = 73088
SGP = 74496
HQ = 75008
HQT = 80640
SGNY = 83456
VB2 = 84480
CY1B = 85888
VBY = 86912
SGP2 = 87936
SGP3 = 88640

ZSTG = 0
ZZ = 2048
RZP = 2304
RZR = 0
Y1P = 2560
HH = 3584
QS = 4992
WRAW = 5344
P1S = 6368
P1A = HH
HHY = 6496
Z_COLS = 7520


class Sem:
    def __init__(self, h):
        self.h = h
        self.n = 0


class E:
    def __init__(self, eng):
        self.e = eng
        self.wr = []
        self.rd = []
        self.waits = {}

    def wait(self, sem, val):
        if val <= 0 or self.waits.get(id(sem), -1) >= val:
            return
        self.e.wait_ge(sem.h, val)
        self.waits[id(sem)] = val

    def drain(self):
        self.e.drain()
        self.wr = []
        self.rd = []

    @staticmethod
    def _rng(a):
        off = a.offset - a.base_partition() * a.ap[0][0]
        span = 1
        for st, n in a.ap[1:]:
            span = max(span, abs(st) * (n - 1) + 1)
        es = 2 if a.dtype == BF16 else 4
        return (a.tensor.name, off * es, (off + span) * es)

    @staticmethod
    def _hit(r, s):
        return r[0] == s[0] and r[1] < s[2] and s[1] < r[2]

    def _pre(self, outs, ins):
        no = [self._rng(a) for a in outs]
        ni = [self._rng(a) for a in ins]
        conflict = any(self._hit(r, s) for r in ni for s in self.wr) or \
            any(self._hit(r, s) for r in no for s in self.wr) or \
            any(self._hit(r, s) for r in no for s in self.rd)
        if conflict:
            self.drain()
        self.wr.extend(no)
        self.rd.extend(ni)

    def cp(self, out, in_):
        self._pre([out], [in_])
        if hasattr(self.e, "tensor_copy"):
            return self.e.tensor_copy(out, in_)
        return self.e.copy(out, in_)

    def ts(self, out, in0, s1, s2=None, op0=ALU.mult, op1=ALU.bypass):
        ins = [in0] + [x for x in (s1, s2) if isinstance(x, bass.AP)]
        self._pre([out], ins)
        if s2 is None:
            return self.e.tensor_single_scalar(out, in0, s1, op=op0)
        return self.e.tensor_scalar(out, in0, s1, s2, op0=op0, op1=op1)

    def tt(self, out, a, b, op):
        self._pre([out], [a, b])
        return self.e.tensor_tensor(out, a, b, op=op)

    def stt(self, out, in0, scalar, in1, op0, op1):
        self._pre([out], [in0, in1])
        return self.e.scalar_tensor_tensor(out, in0, scalar, in1, op0=op0, op1=op1)

    def red(self, out, in_, op=ALU.max, absval=False):
        self._pre([out], [in_])
        return self.e.tensor_reduce(out, in_, axis=AX.X, op=op,
                                    apply_absolute_value=absval)

    def recip(self, out, in_):
        self._pre([out], [in_])
        return self.e.reciprocal(out, in_)

    def mset(self, ap, c):
        self._pre([ap], [])
        return self.e.memset(ap, c)

    def act(self, out, in_, f, bias=0.0, scale=1.0, accum=None):
        ins = [in_] + [x for x in (bias, scale) if isinstance(x, bass.AP)]
        self._pre([out] + ([accum] if accum is not None else []), ins)
        return self.e.activation(out, in_, f, bias=bias, scale=scale,
                                 accum_out=accum)

    def bnstats(self, out, in_):
        self._pre([out], [in_])
        return self.e.bn_stats(out, in_)

    def bnaggr(self, out, in_):
        self._pre([out], [in_])
        return self.e.bn_aggr(out, in_)

    def iota(self, out, pattern, base=0, chm=0):
        self._pre([out], [])
        return self.e.iota(out, pattern=pattern, base=base,
                           channel_multiplier=chm,
                           allow_small_or_imprecise_dtypes=True)

    def aff(self, out, in_, cmp, fill, base, pattern, chm):
        self._pre([out], [in_])
        return self.e.affine_select(out=out, in_=in_, compare_op=cmp, fill=fill,
                                    base=base, pattern=pattern,
                                    channel_multiplier=chm)

    def dma(self, out, in_):
        return self.e.dma_start(out=out, in_=in_)

    def dmat(self, out, in_):
        return self.e.dma_start_transpose(out, in_)

    def mm(self, out, lhsT, rhs, start=True, stop=True):
        return self.e.matmul(out, lhsT, rhs, start=start, stop=stop)

    def nop(self):
        return self.e.engine_nop()


def build(nc, dbg_keys=(), upto="g", seqs=NSEQ, with_p1=True):
    dbg = {}
    x_h = nc.dram_tensor("x", [NSEQ, S, DM], F32, kind="ExternalInput").ap()
    qw_h = nc.dram_tensor("qw", [DM, DM], F32, kind="ExternalInput").ap()
    kw_h = nc.dram_tensor("kw", [DM, DM], F32, kind="ExternalInput").ap()
    vw_h = nc.dram_tensor("vw", [DM, DM], F32, kind="ExternalInput").ap()
    ow_h = nc.dram_tensor("ow", [DM, DM], F32, kind="ExternalInput").ap()
    ob_h = nc.dram_tensor("ob", [DM], F32, kind="ExternalInput").ap()
    us_h = nc.dram_tensor("us", [1], F32, kind="ExternalInput").ap()
    wg_h = nc.dram_tensor("wg", [DI, DM], F32, kind="ExternalInput").ap()
    wu_h = nc.dram_tensor("wu", [DI, DM], F32, kind="ExternalInput").ap()
    wd_h = nc.dram_tensor("wd", [DM, DI], F32, kind="ExternalInput").ap()
    y_h = nc.dram_tensor("y", [NSEQ, S, DM], F32, kind="ExternalOutput").ap()
    wat_h = nc.dram_tensor("wattn_s", [4, DM, DM], BF16, kind="Internal").ap()
    wgq_h = nc.dram_tensor("wgq_s", [DI, DM], BF16, kind="Internal").ap()
    wuq_h = nc.dram_tensor("wuq_s", [DI, DM], BF16, kind="Internal").ap()
    wdq_h = nc.dram_tensor("wdq_s", [DM, DI], BF16, kind="Internal").ap()
    y1_h = nc.dram_tensor("y1_s", [NSEQ, S, DM], F32, kind="Internal").ap()
    for k in dbg_keys:
        shp = {"xT": [128, 16384], "qT": [128, 16384], "kT": [128, 16384],
               "vaug": [128, 16640], "kst": [128, 4096], "vsaug": [128, 4160],
               "att": [128, 16384], "zz": [128, 256], "expl": [128, 2304],
               "y1q": [128, 1024], "h0": [128, 1408], "hq": [128, 2816]}[k]
        dbg[k] = nc.dram_tensor("dbg_" + k, shp, BF16 if k not in ("zz", "h0")
                                else F32, kind="ExternalOutput").ap()

    es = ExitStack()
    A_t = es.enter_context(nc.sbuf_tensor("A", [128, A_COLS], BF16))
    Z_t = es.enter_context(nc.sbuf_tensor("Zf", [128, Z_COLS], F32))
    ident = es.enter_context(nc.sbuf_tensor("ident", [128, 128], BF16))
    maskl = es.enter_context(nc.sbuf_tensor("maskl", [128, 144], BF16))
    csml = es.enter_context(nc.sbuf_tensor("csml", [128, 4], F32))
    lns = es.enter_context(nc.sbuf_tensor("lns", [128, 20], F32))
    ones1 = es.enter_context(nc.sbuf_tensor("ones1", [1, 128], BF16))
    onesf = es.enter_context(nc.sbuf_tensor("onesf", [1, 128], F32))
    ones2f = es.enter_context(nc.sbuf_tensor("ones2f", [65, 128], F32))
    obbf = es.enter_context(nc.sbuf_tensor("obbf", [1, 1024], BF16))
    ussb = es.enter_context(nc.sbuf_tensor("ussb", [1, 1], F32))
    P = [es.enter_context(nc.psum_tensor("P%d" % i, [128, 512], F32))
         for i in range(8)]
    ch_names = ("sT", "sV", "sA", "sG", "cOB", "cWR", "cWS", "cWQ",
                "cXB0", "cXB1", "cX0", "cX1", "cSTG", "cZ", "cWT",
                "cY1", "cY", "cDBG", "cPS", "cZU0", "cZU1")
    sem_h = {k: es.enter_context(nc.semaphore(k)) for k in ch_names}
    (sT, sV, sA, sG, cOB, cWR, cWS, cWQ, cXB0, cXB1, cX0, cX1, cSTG, cZ,
     cWT, cY1, cY, cDBG, cPS, cZU0, cZU1) = (Sem(sem_h[k]) for k in ch_names)
    cZU = (cZU0, cZU1)
    cXB = (cXB0, cXB1)
    cX = (cX0, cX1)

    rec = {k: [] for k in ("sync", "tensor", "vector", "scalar", "gpsimd")}

    def w8(eng, sem, val=None):
        v = sem.n if val is None else val
        rec[eng].append(lambda w, s=sem, v=v: w.wait(s, v))

    def run(eng, fn, inc=None, k=1):
        if inc is not None:
            inc.n += k
        def thunk(w, fn=fn, inc=inc, k=k):
            ins = fn(w)
            if inc is not None:
                ins.then_inc(inc.h, k)
        rec[eng].append(thunk)

    bank_ev = {i: (sV, 0) for i in range(8)}

    def bank_wait(i, eng="tensor"):
        s, v = bank_ev[i]
        w8(eng, s, v)

    def bank_set(i, sem):
        bank_ev[i] = (sem, sem.n)

    Ab = lambda off, wdt: A_t[:, off:off + wdt]
    Zp = lambda off, wdt: Z_t[:, off:off + wdt]

    def blk3(ap):
        return ap.rearrange("p (nb blk) -> p nb blk", blk=BLK)

    def _mk3(t, base, n, step, wdt, lo=0):
        v = t[:, base:base + n * step]
        v = v.rearrange("p (c n) -> p c n", c=n)
        return v[:, :, lo:lo + wdt]

    run("vector", lambda w: w.mset(ident[:], 0.0))
    run("vector", lambda w: w.mset(maskl[:], 1.0))
    run("vector", lambda w: w.mset(ones1[:], 1.0))
    run("vector", lambda w: w.mset(onesf[:], 1.0))
    run("vector", lambda w: w.mset(ones2f[:], 1.0))
    run("vector", lambda w: w.mset(ident[:1, :1], 0.0), inc=sV)
    w8("gpsimd", sV)
    run("gpsimd", lambda w: w.aff(ident[:], ident[:], ALU.not_equal, 1.0, 0,
                                  [[-1, 128]], 1))
    run("gpsimd", lambda w: w.aff(maskl[:], maskl[:], ALU.is_ge, 0.0, 0,
                                  [[1, 144]], -1))
    run("gpsimd", lambda w: w.aff(maskl[:], maskl[:], ALU.is_ge, 0.0, 16,
                                  [[-1, 144]], 1))
    run("gpsimd", lambda w: w.iota(
        Z_t[0:1, ZZ:ZZ + 128].rearrange("o (g r) -> o g r", r=4),
        [[0, 32], [1, 4]], 0, 0))
    run("gpsimd", lambda w: w.ts(Z_t[0:1, ZZ:ZZ + 128], Z_t[0:1, ZZ:ZZ + 128],
                                 0.0, None, op0=ALU.is_gt), inc=sG)
    CONST_G0 = sG.n
    run("sync", lambda w: w.dma(Z_t[0:1, Y1P:Y1P + 1024], ob_h.unsqueeze(0)),
        inc=cOB, k=16)
    run("sync", lambda w: w.dma(ussb[:], us_h.unsqueeze(0)), inc=cOB, k=16)
    w8("vector", cOB)
    run("vector", lambda w: w.cp(obbf[:], Z_t[0:1, Y1P:Y1P + 1024]), inc=sV)
    w8("tensor", cOB)
    w8("tensor", sV)
    run("tensor", lambda w: w.mm(P[0][:, 0:1], onesf[:], ussb[:]), inc=sT)
    w8("scalar", sT)
    run("scalar", lambda w: w.act(csml[:, 0:1], P[0][:, 0:1], AF.Sigmoid), inc=sA)
    w8("vector", sA)
    run("vector", lambda w: w.ts(csml[:, 0:1], csml[:, 0:1], 0.125), inc=sV)
    w8("tensor", sG, CONST_G0)
    w8("tensor", sA)
    run("tensor", lambda w: w.mm(P[0][:, 2:3], Z_t[0:1, ZZ:ZZ + 128],
                                 onesf[0:1, 0:1]), inc=sT)
    w8("vector", sT)
    run("vector", lambda w: w.cp(csml[:, 1:2], P[0][:, 2:3]), inc=sV)
    w8("gpsimd", sV)
    run("gpsimd", lambda w: w.ts(maskl[:], maskl[:], csml[:, 1:2], op0=ALU.mult),
        inc=sG)
    CONST_G = sG.n
    bank_set(0, sV)
    sig8 = csml[:, 0:1]

    for wi, wh in enumerate((qw_h, kw_h, vw_h, ow_h)):
        for it in range(8):
            w8("gpsimd", sV)
            run("gpsimd", lambda w, wh=wh, it=it:
                w.dma(Zp(WRAW, 1024), wh[it * 128:(it + 1) * 128, :]),
                inc=cWR, k=16)
            w8("vector", cWR)
            w8("vector", cPS)
            run("vector", lambda w: w.cp(Ab(P1T, 1024), Zp(WRAW, 1024)), inc=sV)
            w8("gpsimd", sV)
            run("gpsimd", lambda w, wi=wi, it=it:
                w.dma(wat_h[wi, it * 128:(it + 1) * 128, :], Ab(P1T, 1024)),
                inc=cPS, k=16)

    p1_jobs = []
    for src_, dst_ in ((wg_h, wgq_h), (wu_h, wuq_h)):
        for it in range(DI // 128):
            p1_jobs.append((src_, dst_, it, 0, 1024))
    for it in range(DM // 128):
        for off in (0, 1024, 2048):
            p1_jobs.append((wd_h, wdq_h, it, off, min(1024, DI - off)))
    P1_TOTAL = 16 * len(p1_jobs)
    p1_idx = [0]

    p1_pref = [0]
    p1_sA = [0, 0]
    p1_sV = [0, 0]

    def p1_load(i):
        if i >= len(p1_jobs):
            return
        src_, dst_, it, off, pw = p1_jobs[i]
        w8("gpsimd", sA)
        run("gpsimd", lambda w, s=src_, it=it, o=off, pw=pw:
            w.dma(Zp(WRAW, pw), s[it * 128:(it + 1) * 128, o:o + pw]),
            inc=cWR, k=16)
        p1_pref[0] = i + 1

    p1_cap = [len(p1_jobs)]
    p1_phase = [0]

    def p1_tick(n=1):
        for _ in range(n):
            if p1_idx[0] >= min(p1_cap[0], len(p1_jobs)):
                return
            i = p1_idx[0]
            src_, dst_, it, off, pw = p1_jobs[i]
            if p1_phase[0] == 0:
                if p1_pref[0] <= i:
                    p1_load(i)
                w8("scalar", cWR)
                w8("scalar", sV)
                run("scalar", lambda w, pw=pw:
                    w.act(Ab(P1T + 2048, pw), Zp(WRAW, pw), AF.Sign))
                run("scalar", lambda w, pw=pw:
                    w.act(Zp(P1A, pw), Zp(WRAW, pw), AF.Abs), inc=sA)
                w8("vector", sA)
                w8("vector", cWS)
                w8("vector", cPS)
                run("vector", lambda w, pw=pw:
                    quant_v(w, Zp(P1A, pw), Ab(P1T, pw), pw, c_off=P1T + 1024,
                            sgn_ap=Ab(P1T + 2048, pw), qs_off=P1S, vb_off=-1,
                            part=0), inc=sV)
                p1_phase[0] = 1
            else:
                run("vector", lambda w, pw=pw:
                    quant_v(w, Zp(P1A, pw), Ab(P1T, pw), pw, c_off=P1T + 1024,
                            sgn_ap=Ab(P1T + 2048, pw), qs_off=P1S, vb_off=-1,
                            part=1), inc=sV)
                p1_sV[0] = sV.n
                p1_load(i + 1)
                w8("gpsimd", sV, p1_sV[0])
                run("gpsimd", lambda w, d=dst_, it=it, o=off, pw=pw:
                    w.dma(d[it * 128:(it + 1) * 128, o:o + pw], Ab(P1T, pw)),
                    inc=cWS, k=16)
                p1_phase[0] = 0
                p1_idx[0] += 1

    def quant_v(w, sf, dst_bf, pw, c_off=None, sgn_ap=None, qs_off=None,
                vb_off=None, part=None):
        nb = pw // BLK
        qs = QS if qs_off is None else qs_off
        nb_stride = 176 if qs_off is None else 64
        bmax = Zp(qs, nb)
        sv = Zp(qs + nb_stride, nb)
        c = Ab(CCH if c_off is None else c_off, pw)
        sgn = Ab(SGN, pw) if sgn_ap is None else sgn_ap
        vb = (sf if vb_off == -1
              else Ab(VB2 if vb_off is None else vb_off, pw))
        if part != 1:
            w.red(bmax.unsqueeze(2), blk3(sf), op=ALU.max, absval=True)
            w.ts(sv, bmax, 1.0 / 6.0, 1e-12, op0=ALU.mult, op1=ALU.max)
            w.tt(blk3(sgn), blk3(sgn),
                 sv.unsqueeze(2).broadcast_to([128, nb, BLK]),
                 op=ALU.mult)
            w.recip(sv, sv)
            sbc = sv.unsqueeze(2).broadcast_to([128, nb, BLK])
            r = w.tt(blk3(vb), blk3(sf), sbc, op=ALU.mult)
            if part == 0:
                return r
        w.ts(c, vb, 5.0, 2.0, op0=ALU.is_gt, op1=ALU.mult)
        w.ts(dst_bf, vb, 0.25, None, op0=ALU.is_gt)
        w.stt(c, vb, 2.5, c, op0=ALU.is_gt, op1=ALU.add)
        w.stt(dst_bf, vb, 0.75, dst_bf, op0=ALU.is_gt, op1=ALU.add)
        w.stt(c, vb, 3.5, c, op0=ALU.is_gt, op1=ALU.add)
        w.stt(dst_bf, vb, 1.25, dst_bf, op0=ALU.is_gt, op1=ALU.add)
        w.stt(dst_bf, vb, 1.75, dst_bf, op0=ALU.is_gt, op1=ALU.add)
        w.stt(dst_bf, dst_bf, 0.5, c, op0=ALU.mult, op1=ALU.add)
        return w.tt(dst_bf, dst_bf, sgn, op=ALU.mult)

    def qjob_steps(sf, dst_bf, pw, c, sgn, bmax, sv, vb, pre_red=False):
        nb = pw // BLK
        sbc = sv.unsqueeze(2).broadcast_to([128, nb, BLK])
        return ([] if pre_red else [
            lambda w: w.red(bmax.unsqueeze(2), blk3(sf), op=ALU.max,
                            absval=True)]) + [
            lambda w: w.ts(sv, bmax, 1.0 / 6.0, 1e-12, op0=ALU.mult,
                           op1=ALU.max),
            lambda w: w.tt(blk3(sgn), blk3(sgn), sbc, op=ALU.mult),
            lambda w: w.recip(sv, sv),
            lambda w: w.tt(blk3(vb), blk3(sf), sbc, op=ALU.mult),
            lambda w: w.ts(c, vb, 5.0, 2.0, op0=ALU.is_gt, op1=ALU.mult),
            lambda w: w.ts(dst_bf, vb, 0.25, None, op0=ALU.is_gt),
            lambda w: w.stt(c, vb, 2.5, c, op0=ALU.is_gt, op1=ALU.add),
            lambda w: w.stt(dst_bf, vb, 0.75, dst_bf, op0=ALU.is_gt,
                            op1=ALU.add),
            lambda w: w.stt(c, vb, 3.5, c, op0=ALU.is_gt, op1=ALU.add),
            lambda w: w.stt(dst_bf, vb, 1.25, dst_bf, op0=ALU.is_gt,
                            op1=ALU.add),
            lambda w: w.stt(dst_bf, vb, 1.75, dst_bf, op0=ALU.is_gt,
                            op1=ALU.add),
            lambda w: w.stt(dst_bf, dst_bf, 0.5, c, op0=ALU.mult, op1=ALU.add),
            lambda w: w.tt(dst_bf, dst_bf, sgn, op=ALU.mult),
        ]

    def quant_v2(w, jobs):
        steps = [qjob_steps(*j) for j in jobs]
        r = None
        for k in range(max(len(s) for s in steps)):
            for st in steps:
                if k < len(st):
                    r = st[k](w)
        return r

    PH = "abcdefg"
    def author_seq(s):
        w8("gpsimd", sV)
        w8("tensor", sG, CONST_G)
        w8("vector", sG, CONST_G)
        run("vector", lambda w: w.mset(_mk3(A_t, VAUG + 64, 256, 65, 1), 1.0))
        run("vector", lambda w: w.mset(_mk3(A_t, VSAUG + 64, 64, 65, 1), 1.0),
            inc=sV)
        ones_v = sV.n
        cast_done = [0] * NT
        tp_done = [0] * NT
        w8("sync", sV)
        for tc in range(NT):
            buf = XBF + (tc % 2) * 1024
            stg = ZSTG + (tc % 2) * 1024
            if tc >= 2:
                w8("sync", sA, cast_done[tc - 2])
            run("sync", lambda w, s=s, tc=tc, stg=stg:
                w.dma(Zp(stg, 1024), x_h[s, tc * 128:(tc + 1) * 128, :]),
                inc=cXB[tc % 2], k=16)
            w8("scalar", cXB[tc % 2], cXB[tc % 2].n)
            if tc >= 2:
                w8("scalar", sT, tp_done[tc - 2])
            run("scalar", lambda w, buf=buf, stg=stg:
                w.cp(Ab(buf, 1024), Zp(stg, 1024)), inc=sA)
            cast_done[tc] = sA.n
            w8("tensor", sA, cast_done[tc])
            for half in range(2):
                b = 4 + (2 * tc + half) % 4
                bank_wait(b)
                for j in range(4):
                    fb = half * 4 + j
                    run("tensor", lambda w, buf=buf, fb=fb, b=b, j=j:
                        w.mm(P[b][:, j * 128:(j + 1) * 128],
                             Ab(buf + fb * 128, 128), ident[:]),
                        inc=(sT if j == 3 else None))
                w8("scalar", sT)
                run("scalar", lambda w, tc=tc, half=half, b=b:
                    w.cp(_mk3(A_t, XT + half * 4 * 2048 + tc * 128, 4, 2048, 128),
                         _mk3(P[b], 0, 4, 128, 128)),
                    inc=sA)
                bank_set(b, sA)
            tp_done[tc] = sT.n
        xT_done_v = sA.n
        if s == 0 and "xT" in dbg:
            w8("sync", sA, xT_done_v)
            run("sync", lambda w: w.dma(dbg["xT"], Ab(XT, 16384)), inc=cDBG, k=16)
            w8("sync", cDBG)
            w8("vector", cDBG)
            w8("tensor", cDBG)

        if PH.index(upto) < PH.index("b"):
            return
        def load_w(wi):
            w8("sync", sT)
            w8("sync", cPS)
            run("sync", lambda w, wi=wi:
                w.dma(_mk3(A_t, WQKV, 8, 1024, 1024),
                      wat_h[wi].rearrange("(c p) n -> p c n", p=128)),
                inc=cWQ, k=16)
            return cWQ.n

        wch = (cXB0, cXB1, cX0, cX1, cZU0, cZU1, cSTG, cZ)

        def load_w_chunks(wi):
            w8("sync", sT)
            w8("sync", cPS)
            lds = []
            for dqc in range(8):
                run("sync", lambda w, wi=wi, dqc=dqc:
                    w.dma(_mk3(A_t, WQKV, 8, 1024, 128, lo=dqc * 128),
                          wat_h[wi][:, dqc * 128:(dqc + 1) * 128]
                          .rearrange("(c p) n -> p c n", p=128)),
                    inc=wch[dqc], k=16)
                lds.append(wch[dqc].n)
            return lds

        for wi, (wh, dst, scale) in enumerate(
                ((0, QT, True), (1, KT, False))):
            lds = load_w_chunks(wh)
            w8("tensor", sA, xT_done_v)
            for dqc in range(8):
                b0 = 4 * (dqc % 2)
                w8("tensor", wch[dqc], lds[dqc])
                for qt in range(NQT):
                    bank_wait(b0 + qt)
                for fc in range(8):
                    for qt in range(NQT):
                        run("tensor", lambda w, fc=fc, dqc=dqc, qt=qt, b0=b0:
                            w.mm(P[b0 + qt][:],
                                 Ab(WQKV + fc * 1024 + dqc * 128, 128),
                                 Ab(XT + fc * 2048 + qt * 512, 512),
                                 start=(fc == 0), stop=(fc == 7)),
                            inc=(sT if (fc == 7 and qt == NQT - 1) else None))
                w8("scalar", sT)
                for qt in range(NQT):
                    if scale:
                        run("scalar", lambda w, dst=dst, dqc=dqc, qt=qt, b0=b0:
                            w.act(Ab(dst + dqc * 2048 + qt * 512, 512),
                                  P[b0 + qt][:], AF.Copy, scale=sig8), inc=sA)
                    else:
                        run("scalar", lambda w, dst=dst, dqc=dqc, qt=qt, b0=b0:
                            w.cp(Ab(dst + dqc * 2048 + qt * 512, 512),
                                 P[b0 + qt][:]), inc=sA)
                    bank_set(b0 + qt, sA)
                p1_tick(1)
        qk_done_v = sA.n
        w8("vector", sA, qk_done_v)
        for hp in range(8):
            run("vector", lambda w, hp=hp:
                w.cp(Ab(KST + hp * 512, 512),
                     A_t[:, KT + hp * 2048:KT + (hp + 1) * 2048:4]), inc=sV)
        kst_done_v = sV.n
        ldv = load_w(2)
        w8("tensor", cWQ, ldv)
        for tc in range(NT):
            vb0 = 4 * (tc % 2)
            for b in (0, 1):
                bank_wait(vb0 + b)
            for fc in range(8):
                for b in (0, 1):
                    run("tensor", lambda w, tc=tc, fc=fc, b=b, vb0=vb0:
                        w.mm(P[vb0 + b][:],
                             Ab(XT + fc * 2048 + tc * 128, 128),
                             Ab(WQKV + fc * 1024 + b * 512, 512),
                             start=(fc == 0), stop=(fc == 7)),
                        inc=(sT if (fc == 7 and b == 1) else None))
            w8("scalar", sT)
            for b in (0, 1):
                run("scalar", lambda w, tc=tc, b=b, vb0=vb0:
                    w.cp(_mk3(A_t, VAUG + tc * 1040 + b * 8 * 65, 8, 65, 64),
                         _mk3(P[vb0 + b], 0, 8, 64, 64)), inc=sA)
                bank_set(vb0 + b, sA)
        for c in range(4):
            sb0 = 2 if c % 2 == 0 else 6
            for bi in (0, 1):
                bank_wait(sb0 + bi)
            for fc in range(8):
                for bi in (0, 1):
                    run("tensor", lambda w, c=c, fc=fc, bi=bi, sb0=sb0:
                        w.mm(P[sb0 + bi][:],
                             A_t[:, XT + fc * 2048 + 512 * c:
                                 XT + fc * 2048 + 512 * (c + 1):4],
                             Ab(WQKV + fc * 1024 + bi * 512, 512),
                             start=(fc == 0), stop=(fc == 7)),
                        inc=(sT if (fc == 7 and bi == 1) else None))
            w8("scalar", sT)
            for bi in (0, 1):
                run("scalar", lambda w, c=c, bi=bi, sb0=sb0:
                    w.cp(_mk3(A_t, VSAUG + c * 1040 + bi * 8 * 65, 8, 65, 64),
                         _mk3(P[sb0 + bi], 0, 8, 64, 64)), inc=sA)
                bank_set(sb0 + bi, sA)
        qkv_done_v = sA.n
        qkv_done_sem = sA
        qkv_done_t = sT.n
        if s == 0:
            for key, off, wdt in (("qT", QT, 16384), ("kT", KT, 16384),
                                  ("vaug", VAUG, 16640), ("kst", KST, 4096),
                                  ("vsaug", VSAUG, 4160)):
                if key in dbg:
                    w8("sync", sV, qkv_done_v)
                    run("sync", lambda w, key=key, off=off, wdt=wdt:
                        w.dma(dbg[key], Ab(off, wdt)), inc=cDBG, k=16)
                    w8("sync", cDBG)
                    w8("vector", cDBG)
                    w8("tensor", cDBG)

        if PH.index(upto) < PH.index("c"):
            return
        zpack_dmas = []
        stg_dma = [0]
        w8("tensor", sV, max(ones_v, kst_done_v))
        w8("tensor", sA, qkv_done_v)
        lm = [0] * (NH + 1)
        lmask_n = [0] * (NH + 1)

        def local_tile(h, j):
            hp, hr = h // 2, h % 2
            r0 = 64 * hr
            eb = EXPL + (h % 2) * EXPL_STRIDE
            qlo = 128 * j - 8
            ust = 8 if j == 0 else 0
            uend = 136 if j == NT - 1 else 144
            wdt = uend - ust
            b = 4 + (j % 2)
            bank_wait(b)
            run("tensor", lambda w, hp=hp, r0=r0, j=j, qlo=qlo, ust=ust,
                wdt=wdt, b=b:
                w.mm(P[b][:, 0:wdt],
                     A_t[r0:r0 + 64, KT + hp * 2048 + 128 * j:
                         KT + hp * 2048 + 128 * (j + 1)],
                     A_t[r0:r0 + 64, QT + hp * 2048 + qlo + ust:
                         QT + hp * 2048 + qlo + ust + wdt]),
                inc=sT)
            w8("scalar", sT)
            run("scalar", lambda w, j=j, ust=ust, wdt=wdt, b=b, eb=eb:
                w.act(Ab(eb + j * 144 + ust, wdt), P[b][:, 0:wdt], AF.Exp),
                inc=sA)
            bank_set(b, sA)
            w8("vector", sA)
            run("vector", lambda w, j=j, ust=ust, wdt=wdt, eb=eb:
                w.tt(Ab(eb + j * 144 + ust, wdt),
                     Ab(eb + j * 144 + ust, wdt),
                     maskl[:, ust:ust + wdt], op=ALU.mult), inc=sV)
            lmask_n[h] = sV.n

        for j in range(NT):
            local_tile(0, j)
        lm[0] = sV.n

        for h in range(NH):
            hp, hr = h // 2, h % 2
            r0 = 64 * hr
            eb = EXPL + (h % 2) * EXPL_STRIDE
            for qt in range(NQT):
                sb = (0, 1, 6, 7)
                ob = 2 + (qt % 2)
                bank_wait(ob)
                exp_a = [0] * 4
                for c in range(4):
                    b = sb[c]
                    bank_wait(b)
                    run("tensor", lambda w, hp=hp, r0=r0, c=c, qt=qt, b=b:
                        w.mm(P[b][:],
                             Ab(KST + hp * 512 + 128 * c, 128)
                             [r0:r0 + 64, :],
                             A_t[r0:r0 + 64, QT + hp * 2048 + qt * 512:
                                 QT + hp * 2048 + qt * 512 + 512]),
                        inc=sT)
                    w8("scalar", sT)
                    run("scalar", lambda w, c=c, b=b:
                        w.act(Ab(EXPS + c * 512, 512), P[b][:], AF.Exp), inc=sA)
                    exp_a[c] = sA.n
                    bank_set(b, sA)
                if h + 1 < NH:
                    for j in range(4 * qt, 4 * qt + 4):
                        local_tile(h + 1, j)
                for c in range(4):
                    w8("tensor", sA, exp_a[c])
                    run("tensor", lambda w, c=c, h=h, ob=ob:
                        w.mm(P[ob][0:65, :],
                             Ab(VSAUG + (c * 16 + h) * 65, 65),
                             Ab(EXPS + c * 512, 512),
                             start=(c == 0), stop=False),
                        inc=sT)
                w8("tensor", sV, lm[h])
                q0, q1 = 512 * qt, 512 * qt + 512
                hits = []
                for j in range(max(0, 4 * qt - 1), min(NT, 4 * qt + 5)):
                    qlo = 128 * j - 8
                    ust = 8 if j == 0 else 0
                    uend = 136 if j == NT - 1 else 144
                    a = max(qlo + ust, q0)
                    bnd = min(qlo + uend, q1)
                    if a < bnd:
                        hits.append((j, qlo, a, bnd))
                for idx, (j, qlo, a, bnd) in enumerate(hits):
                    last = idx == len(hits) - 1
                    run("tensor", lambda w, j=j, qlo=qlo, a=a, bnd=bnd, h=h,
                        q0=q0, last=last, eb=eb, ob=ob:
                        w.mm(P[ob][0:65, a - q0:bnd - q0],
                             Ab(VAUG + (j * 16 + h) * 65, 65),
                             Ab(eb + j * 144 + (a - qlo), bnd - a),
                             start=False, stop=last),
                        inc=(sT if last else None))
                pv_t = sT.n
                w8("vector", sT, pv_t)
                if hr == 0:
                    run("vector", lambda w, hp=hp, qt=qt, ob=ob:
                        w.cp(A_t[0:64, XT + hp * 2048 + qt * 512:
                                 XT + hp * 2048 + qt * 512 + 512],
                             P[ob][0:64, :]), inc=sV)
                else:
                    w8("vector", cSTG, stg_dma[0])
                    run("vector", lambda w, qt=qt, ob=ob:
                        w.cp(A_t[0:65, STG:STG + 512], P[ob][0:65, :]), inc=sV)
                    stg_v = sV.n
                    w8("sync", sV, stg_v)
                    run("sync", lambda w, hp=hp, qt=qt:
                        w.dma(A_t[64:128, XT + hp * 2048 + qt * 512:
                                  XT + hp * 2048 + qt * 512 + 512],
                              A_t[0:64, STG:STG + 512]), inc=cSTG, k=16)
                    stg_dma[0] = cSTG.n
                src_z = (P[ob][64:65, :]) if hr == 0 else (A_t[64:65, STG:STG + 512])
                run("vector", lambda w, qt=qt, src_z=src_z:
                    w.cp(Z_t[64:65, ZSTG + qt * 512:ZSTG + qt * 512 + 512],
                         src_z), inc=sV)
                zrow_v = sV.n
                bank_set(ob, sV)
                if qt % 2 == 1:
                    p1_tick(1)
            if h + 1 < NH:
                lm[h + 1] = lmask_n[h + 1]
            w8("sync", sV, zrow_v)
            run("sync", lambda w, h=h:
                w.dma(Z_t[8 * h:8 * h + 8, ZZ:ZZ + 256],
                      Z_t[64:65, ZSTG:ZSTG + 2048]), inc=cZ, k=16)
            zpack_dmas.append(cZ.n)
            w8("vector", cZ, zpack_dmas[-1])
        att_unnorm_v = sV.n
        att_evac_a = sA.n

        if PH.index(upto) < PH.index("d"):
            return
        w8("vector", cZ, zpack_dmas[-1])
        w8("vector", cSTG)
        run("vector", lambda w: w.recip(Zp(RZP, 256), Zp(ZZ, 256)), inc=sV)
        rzp_v = sV.n
        w8("vector", sA, att_evac_a)
        if s == 0 and "zz" in dbg:
            w8("sync", sV)
            run("sync", lambda w: w.dma(dbg["zz"], Zp(ZZ, 256)), inc=cDBG, k=16)
            w8("sync", cDBG)
        rz_mm_t = [0, 0, 0]
        czu3 = (cZU0, cZU1, cSTG)

        def rz_unpack(h):
            s3 = h % 3
            w8("sync", sV, rzp_v)
            w8("sync", sT, rz_mm_t[s3])
            run("sync", lambda w, h=h, s3=s3:
                w.dma(Z_t[32 * s3:32 * s3 + 1, RZR:RZR + 2048],
                      Z_t[8 * h:8 * h + 8, RZP:RZP + 256]), inc=czu3[s3], k=16)
            return czu3[s3].n

        up_done = [0] * (NH + 2)
        up_done[0] = rz_unpack(0)
        up_done[1] = rz_unpack(1)
        for hp in range(8):
            for b in range(4, 8):
                bank_wait(b)
            for hr in range(2):
                h = 2 * hp + hr
                if h + 2 < NH:
                    up_done[h + 2] = rz_unpack(h + 2)
                s3 = h % 3
                w8("tensor", czu3[s3], up_done[h])
                for qt in range(NQT):
                    run("tensor", lambda w, qt=qt, hr=hr, s3=s3:
                        w.mm(P[4 + qt][64 * hr:64 * hr + 64, :],
                             ones2f[32 * s3:32 * s3 + 1, 0:64],
                             Z_t[32 * s3:32 * s3 + 1,
                                 RZR + qt * 512:RZR + qt * 512 + 512]),
                        inc=(sT if qt == NQT - 1 else None))
                rz_mm_t[s3] = sT.n
            w8("vector", sT)
            for qt in range(NQT):
                run("vector", lambda w, hp=hp, qt=qt:
                    w.tt(Ab(XT + hp * 2048 + qt * 512, 512),
                         Ab(XT + hp * 2048 + qt * 512, 512),
                         P[4 + qt][:], op=ALU.mult), inc=sV)
                bank_set(4 + qt, sV)
            p1_tick(1)
        att_done_v = sV.n
        if s == 0 and "att" in dbg:
            w8("sync", sV)
            run("sync", lambda w: w.dma(dbg["att"], Ab(XT, 16384)), inc=cDBG, k=16)
            w8("sync", cDBG)
            w8("tensor", cDBG)
            w8("vector", cDBG)

        if PH.index(upto) < PH.index("e"):
            return
        ldo = load_w(3)
        w8("tensor", cWQ, ldo)
        w8("tensor", sV, att_done_v)
        for tc in range(NT):
            bsel = (0, 1) if tc % 2 == 0 else (2, 3)
            xe = ZSTG + (tc % 2) * 1024
            w8("sync", sV, att_done_v)
            w8("sync", cZ)
            w8("sync", cZU0)
            w8("sync", cZU1)
            run("sync", lambda w, s=s, tc=tc, xe=xe:
                w.dma(Zp(xe, 1024), x_h[s, tc * 128:(tc + 1) * 128, :]),
                inc=cX[tc % 2], k=16)
            xload = cX[tc % 2].n
            for b in bsel:
                bank_wait(b)
            for hp in range(8):
                for bi, b in enumerate(bsel):
                    run("tensor", lambda w, hp=hp, tc=tc, b=b, bi=bi:
                        w.mm(P[b][:],
                             Ab(XT + hp * 2048 + tc * 128, 128),
                             Ab(WQKV + hp * 1024 + bi * 512, 512),
                             start=(hp == 0), stop=False))
            for bi, b in enumerate(bsel):
                run("tensor", lambda w, b=b, bi=bi:
                    w.mm(P[b][:], ones1[:], obbf[:, bi * 512:bi * 512 + 512],
                         start=False, stop=True),
                    inc=(sT if bi == 1 else None))
            w8("vector", sT)
            w8("vector", cX[tc % 2], xload)
            w8("vector", cY1)
            w8("vector", cY)
            for bi, b in enumerate(bsel):
                run("vector", lambda w, b=b, bi=bi, xe=xe:
                    w.tt(Zp(Y1P + bi * 512, 512), P[b][:],
                         Zp(xe + bi * 512, 512), op=ALU.add), inc=sV)
                bank_set(b, sV)
            ln_store(s, tc, y1_h, cY1)
            p1_tick(1)
        y1_done_g[s] = cY1.n
        return

    def author_wload():
        while p1_idx[0] < len(p1_jobs):
            p1_tick(1)
        w8("sync", sV)
        w8("sync", sT)
        w8("sync", cWS, p1_stores)
        for ic in range(22):
            run("sync", lambda w, ic=ic:
                w.dmat(Ab(WDT + ic * 1024, 1024), wdq_h[:, ic * 128:(ic + 1) * 128]),
                inc=cWT, k=16)
        for fc in range(1, 8):
            run("sync", lambda w, fc=fc:
                w.dmat(Ab(WUT + fc * 2816, 2816), wuq_h[:, fc * 128:(fc + 1) * 128]),
                inc=cWT, k=16)
        for fc in range(8):
            run("sync", lambda w, fc=fc:
                w.dmat(Ab(WGT + fc * 2816, 2816), wgq_h[:, fc * 128:(fc + 1) * 128]),
                inc=cWT, k=16)
        run("sync", lambda w:
            w.dmat(Ab(WUT, 2816), wuq_h[:, 0:128]), inc=cWT, k=16)
        wts_loaded_g[0] = cWT.n

    def author_ffn(s):
        y1_done = y1_done_g[s]
        w8("tensor", cWT, wts_loaded_g[0])
        y1qt_v = [0] * (NT + 1)
        y1q_rd = [0, 0]
        hq_v = [0] * (NT + 1)
        hq_rd = [0, 0]
        xe_ld = [None] * (NT + 1)

        y1_sab = [0] * (NT + 1)

        def stageQ_load(tc):
            p = tc % 2
            xe = (ZSTG, ZSTG + 1024, WRAW)[tc % 3]
            w8("sync", sV)
            w8("sync", cY1, y1_done)
            run("sync", lambda w, s=s, tc=tc, xe=xe:
                w.dma(Zp(xe, 1024), y1_h[s, tc * 128:(tc + 1) * 128, :]),
                inc=cX[p], k=16)
            xe_ld[tc] = (tc % 3, p, cX[p].n)

        def y1_signabs(tc, gate_v):
            b3, p, ld = xe_ld[tc]
            xe = (ZSTG, ZSTG + 1024, WRAW)[b3]
            w8("vector", cX[p], ld)
            run("vector", lambda w, xe=xe:
                w.red(Zp(QS + 88, 64).unsqueeze(2), blk3(Zp(xe, 1024)),
                      op=ALU.max, absval=True), inc=sV)
            w8("scalar", cX[p], ld)
            w8("scalar", sV, gate_v)
            run("scalar", lambda w, xe=xe:
                w.act(Ab(SGNY, 1024), Zp(xe, 1024), AF.Sign))
            run("scalar", lambda w, xe=xe:
                w.act(Zp(HHY, 1024), Zp(xe, 1024), AF.Abs), inc=sA)
            y1_sab[tc] = sA.n

        def y1_job(tc):
            p = tc % 2
            return (Zp(HHY, 1024), Ab(Y1Q + p * 1024, 1024), 1024,
                    Ab(CY1B, 1024), Ab(SGNY, 1024),
                    Zp(QS + 88, 64), Zp(QS + 264, 64), Ab(VBY, 1024), True)

        def stageQ_a(tc):
            stageQ_load(tc)
            y1_signabs(tc, sV.n)
            w8("vector", sA, y1_sab[tc])
            w8("vector", sT, y1q_rd[tc % 2])
            run("vector", lambda w, tc=tc:
                quant_v2(w, [y1_job(tc)]), inc=sV)
            return sV.n

        def stageQ_b(tc, q_v):
            p = tc % 2
            w8("tensor", sV, q_v)
            for half in range(2):
                b = 6 + half
                bank_wait(b)
                for j in range(4):
                    fb = half * 4 + j
                    run("tensor", lambda w, fb=fb, b=b, j=j, p=p:
                        w.mm(P[b][:, j * 128:(j + 1) * 128],
                             Ab(Y1Q + p * 1024 + fb * 128, 128), ident[:]),
                        inc=(sT if j == 3 else None))
                w8("scalar", sT)
                run("scalar", lambda w, half=half, b=b, p=p:
                    w.cp(Ab(Y1QT + p * 1024 + half * 512, 512), P[b][:]), inc=sA)
                bank_set(b, sA)
            y1q_rd[p] = sT.n
            y1qt_v[tc] = sA.n

        qv0 = stageQ_a(0)
        stageQ_b(0, qv0)

        def M1(tc, ytc=None):
            p = tc % 2
            jq = [0]
            PIECES = ((0, 512), (512, 512), (1024, 384))

            def sign_abs_slice(poff, pw, gate_v):
                w8("scalar", sV, gate_v)
                run("scalar", lambda w, poff=poff, pw=pw:
                    w.act(Ab(SGN + poff, pw), Zp(HH + poff, pw), AF.Sign))
                run("scalar", lambda w, poff=poff, pw=pw:
                    w.act(Zp(HH + poff, pw), Zp(HH + poff, pw), AF.Abs), inc=sA)

            mult_v = [0, 0, 0]
            mult_g = [0] * 6
            hq0_v = [0]
            for half in range(2):
                hoff = half * 1408
                if half == 1 and ytc is not None:
                    y1_signabs(ytc, mult_v[2])
                for pi, (poff, pw) in enumerate(PIECES):
                    po = hoff + poff
                    bg = 0 if pi % 2 == 0 else 2
                    bu = 1 if pi % 2 == 0 else 3
                    for b in (bg, bu):
                        bank_wait(b)
                    w8("tensor", sA, y1qt_v[tc])
                    for fc in range(8):
                        run("tensor", lambda w, fc=fc, po=po, pw=pw, bg=bg, p=p:
                            w.mm(P[bg][:, 0:pw], Ab(Y1QT + p * 1024 + fc * 128, 128),
                                 Ab(WGT + fc * 2816 + po, pw),
                                 start=(fc == 0), stop=(fc == 7)))
                        run("tensor", lambda w, fc=fc, po=po, pw=pw, bu=bu, p=p:
                            w.mm(P[bu][:, 0:pw], Ab(Y1QT + p * 1024 + fc * 128, 128),
                                 Ab(WUT + fc * 2816 + po, pw),
                                 start=(fc == 0), stop=(fc == 7)),
                            inc=(sT if fc == 7 else None))
                    gpi = half * 3 + pi
                    sgp = (SGP, SGP2, SGP3)[gpi % 3]
                    w8("scalar", sT)
                    if gpi >= 3:
                        w8("scalar", sV, mult_g[gpi - 3])
                    run("scalar", lambda w, pw=pw, bg=bg, sgp=sgp:
                        w.act(Ab(sgp, pw), P[bg][:, 0:pw], AF.Silu), inc=sA)
                    silu_a = sA.n
                    bank_set(bg, sA)
                    if pi >= 1:
                        pv_off, pv_w = PIECES[pi - 1]
                        gate = mult_v[pi - 1]
                        if half == 1:
                            gate = max(gate, hq0_v[0])
                        sign_abs_slice(pv_off, pv_w, gate)
                    w8("vector", sA, silu_a)
                    run("vector", lambda w, poff=poff, pw=pw, bu=bu, sgp=sgp:
                        w.tt(Zp(HH + poff, pw), Ab(sgp, pw), P[bu][:, 0:pw],
                             op=ALU.mult), inc=sV)
                    run("vector", lambda w, poff=poff, pw=pw:
                        w.red(Zp(QS + poff // 16, pw // 16).unsqueeze(2),
                              blk3(Zp(HH + poff, pw)), op=ALU.max,
                              absval=True), inc=sV)
                    mult_v[pi] = sV.n
                    mult_g[gpi] = sV.n
                    bank_set(bu, sV)
                gate = mult_v[2]
                if half == 1:
                    gate = max(gate, hq0_v[0])
                sign_abs_slice(PIECES[2][0], PIECES[2][1], gate)
                w8("vector", sA)
                w8("vector", sT, hq_rd[p])
                hjob = (Zp(HH, 1408), Ab(HQ + p * 2816 + hoff, 1408), 1408,
                        Ab(CCH, 1408), Ab(SGN, 1408),
                        Zp(QS, 88), Zp(QS + 176, 88), Ab(VB2, 1408), True)
                if half == 1 and ytc is not None:
                    w8("vector", sT, y1q_rd[ytc % 2])
                    run("vector", lambda w, hjob=hjob, ytc=ytc:
                        quant_v2(w, [hjob, y1_job(ytc)]), inc=sV)
                    jq[0] = sV.n
                else:
                    run("vector", lambda w, hjob=hjob:
                        quant_v2(w, [hjob]), inc=sV)
                    hq0_v[0] = sV.n
            hq_v[tc] = sV.n
            return jq[0]

        def M2a(tc):
            p = tc % 2
            w8("tensor", sV, hq_v[tc])
            done = 0
            bsel_t = 0
            while done < 22:
                cnt = min(4, 22 - done)
                b = 6 + (bsel_t % 2)
                bsel_t += 1
                bank_wait(b)
                for j in range(cnt):
                    ic = done + j
                    run("tensor", lambda w, ic=ic, b=b, j=j, p=p:
                        w.mm(P[b][:, j * 128:(j + 1) * 128],
                             Ab(HQ + p * 2816 + ic * 128, 128), ident[:]),
                        inc=(sT if j == cnt - 1 else None))
                w8("scalar", sT)
                run("scalar", lambda w, done=done, cnt=cnt, b=b:
                    w.cp(Ab(HQT + done * 128, cnt * 128), P[b][:, 0:cnt * 128]),
                    inc=sA)
                bank_set(b, sA)
                done += cnt
            hq_rd[p] = sT.n
            hqt_v = sA.n
            for b in (4, 5):
                bank_wait(b)
            w8("tensor", sA, hqt_v)
            for ic in range(22):
                for bi, b in enumerate((4, 5)):
                    run("tensor", lambda w, ic=ic, b=b, bi=bi:
                        w.mm(P[b][:], Ab(HQT + ic * 128, 128),
                             Ab(WDT + ic * 1024 + bi * 512, 512),
                             start=(ic == 0), stop=(ic == 21)),
                        inc=(sT if (ic == 21 and bi == 1) else None))

        def M2b(tc):
            b3, pe, ld = xe_ld[tc]
            xe = (ZSTG, ZSTG + 1024, WRAW)[b3]
            w8("vector", sT)
            w8("vector", cX[pe], ld)
            w8("vector", cY)
            w8("vector", cY1)
            for bi, b in enumerate((4, 5)):
                run("vector", lambda w, b=b, bi=bi, xe=xe:
                    w.tt(Zp(Y1P + bi * 512, 512), P[b][:],
                         Zp(xe + bi * 512, 512), op=ALU.add), inc=sV)
                bank_set(b, sV)
            ln_store(s, tc, y_h, cY, presilu=True)

        for tc in range(NT + 1):
            nxt = tc + 1 if tc + 1 <= NT - 1 else None
            if nxt is not None:
                stageQ_load(nxt)
            if tc <= NT - 1:
                qv = M1(tc, ytc=nxt)
            if tc >= 1:
                M2a(tc - 1)
            if nxt is not None:
                stageQ_b(nxt, qv)
            if tc >= 1:
                M2b(tc - 1)

    def ln_store(s, tc, dst_h, ch, presilu=False):
        run("vector", lambda w: w.bnstats(lns[:, 0:6], Zp(Y1P, 512)))
        run("vector", lambda w: w.bnstats(lns[:, 6:12], Zp(Y1P + 512, 512)))
        run("vector", lambda w: w.bnaggr(lns[:, 12:14], lns[:, 0:12]))
        run("vector", lambda w: w.ts(lns[:, 14:15], lns[:, 13:14], EPS, None,
                                     op0=ALU.add), inc=sV)
        w8("scalar", sV)
        run("scalar", lambda w: w.act(lns[:, 15:16], lns[:, 14:15], AF.Sqrt),
            inc=(None if presilu else sA))
        if presilu:
            run("scalar", lambda w:
                w.act(lns[0:1, 17:18], lns[0:1, 14:15], AF.Silu), inc=sA)
        w8("vector", sA)
        run("vector", lambda w: w.recip(lns[:, 16:17], lns[:, 15:16]))
        run("vector", lambda w:
            w.ts(Zp(Y1P, 1024), Zp(Y1P, 1024), lns[:, 12:13], lns[:, 16:17],
                 op0=ALU.subtract, op1=ALU.mult), inc=sV)
        w8("sync", sV)
        run("sync", lambda w, s=s, tc=tc, dst_h=dst_h:
            w.dma(dst_h[s, tc * 128:(tc + 1) * 128, :], Zp(Y1P, 1024)),
            inc=ch, k=16)

    p1_stores = P1_TOTAL if with_p1 else 0
    if not with_p1:
        p1_idx[0] = len(p1_jobs)
    y1_done_g = [0] * NSEQ
    wts_loaded_g = [0]
    for _s in range(seqs):
        p1_cap[0] = 34 if (_s == 0 and seqs > 1) else len(p1_jobs)
        author_seq(_s)
    if PH.index(upto) >= PH.index("f"):
        author_wload()
    if PH.index(upto) >= PH.index("g"):
        for _s in range(seqs):
            author_ffn(_s)

    for _s in (cOB, cWR, cWS, cWQ, cXB0, cXB1, cX0, cX1, cSTG, cZ, cWT,
               cY1, cY, cDBG, cPS, cZU0, cZU1, sV, sT, sA, sG):
        w8("sync", _s)

    with nc.Block() as block:
        @block.sync
        def _(eng):
            w = E(eng)
            for fn in rec["sync"]:
                fn(w)

        @block.tensor
        def _(eng):
            w = E(eng)
            for fn in rec["tensor"]:
                fn(w)

        @block.vector
        def _(eng):
            w = E(eng)
            for fn in rec["vector"]:
                fn(w)

        @block.scalar
        def _(eng):
            w = E(eng)
            for fn in rec["scalar"]:
                fn(w)

        @block.gpsimd
        def _(eng):
            w = E(eng)
            for fn in rec["gpsimd"]:
                fn(w)

    es.close()
    return nc


def kernel(L0_attn_x, L0_attn_q_w, L0_attn_k_w, L0_attn_v_w, L0_attn_o_w,
           L0_attn_o_b, L0_attn_unity_scale, L0_ffn_w_gate, L0_ffn_w_up,
           L0_ffn_w_down):
    x = np.ascontiguousarray(np.asarray(L0_attn_x, dtype=np.float32))
    args = {
        "qw": np.ascontiguousarray(np.asarray(L0_attn_q_w, np.float32)),
        "kw": np.ascontiguousarray(np.asarray(L0_attn_k_w, np.float32)),
        "vw": np.ascontiguousarray(np.asarray(L0_attn_v_w, np.float32)),
        "ow": np.ascontiguousarray(np.asarray(L0_attn_o_w, np.float32)),
        "ob": np.ascontiguousarray(np.asarray(L0_attn_o_b, np.float32)),
        "us": np.asarray(L0_attn_unity_scale, np.float32).reshape(1),
        "wg": np.ascontiguousarray(np.asarray(L0_ffn_w_gate, np.float32)),
        "wu": np.ascontiguousarray(np.asarray(L0_ffn_w_up, np.float32)),
        "wd": np.ascontiguousarray(np.asarray(L0_ffn_w_down, np.float32)),
    }
    nc = bass.Bass("TRN2", target_bir_lowering=False)
    build(nc)
    in_maps = [dict(args, x=np.ascontiguousarray(x[2 * c:2 * c + 2]))
               for c in range(8)]
    res = run_bass_kernel_spmd(nc, in_maps, core_ids=list(range(8)))
    return np.concatenate([r["y"] for r in res.results], axis=0)
